# version 04
# baseline (speedup 1.0000x reference)
import os
import numpy as np
import ml_dtypes

import concourse.bass as bass
import concourse.mybir as mybir
from concourse.bass_utils import run_bass_kernel_spmd

F32 = mybir.dt.float32
BF16 = mybir.dt.bfloat16
AF = mybir.ActivationFunctionType
ALU = mybir.AluOpType

N_CORES = 8
D = 4096
KT = 32
TPC = 2048
NCH = 4
T = 512
OT = 32
HKV = 8
ENGS = ("tensor", "scalar", "vector", "sync", "gpsimd")


def _mask_sel():
    import jax
    import jax.numpy as jnp

    use64 = bool(os.environ.get("BASS_SIM_FP64"))
    if use64:
        jax.config.update("jax_enable_x64", True)
    cpu = jax.devices("cpu")[0]
    with jax.default_device(cpu):
        key = jax.random.key(42, impl="threefry2x32")
        dt = jnp.float64 if use64 else jnp.float32
        u = np.asarray(jax.random.uniform(key, (4, 4096, 4096), dtype=dt))
    sel = (u * 4).astype(np.uint8)
    return sel.reshape(16384, 4096)


class Sched:

    def __init__(self, sems):
        self.sems = sems
        self.prog = {e: [] for e in ENGS}
        self.cnt = {k: 0 for k in sems}
        self.last_wait = {e: {k: 0 for k in sems} for e in ENGS}

    def wait(self, eng, sem, val):
        if val <= 0:
            return
        assert val <= self.cnt[sem], f"wait {sem}>={val} > produced {self.cnt[sem]}"
        if self.last_wait[eng][sem] >= val:
            return
        self.last_wait[eng][sem] = val
        h = self.sems[sem]
        self.prog[eng].append(lambda e, h=h, v=val: e.wait_ge(h, v))

    def inc(self, sem, by):
        self.cnt[sem] += by
        return self.cnt[sem]

    def emit(self, eng, make, sem=None, by=1, fence=None):
        if eng in ("vector", "scalar"):
            self.prog[eng].append(lambda e: e.drain())
        if sem is None:
            self.prog[eng].append(make)
            return None
        n = self.inc(sem, by)
        h = self.sems[sem]
        self.prog[eng].append(lambda e, h=h, b=by: make(e).then_inc(h, b))
        if fence is None:
            fence = sem in ("qS", "qG")
        if fence and sem in ("qS", "qG"):
            self.wait(eng, sem, n)
        return n


def build_program():
    nc = bass.Bass("TRN2", target_bir_lowering=False, debug=False)

    xt = nc.dram_tensor("xt", [D, TPC], F32, kind="ExternalInput")
    w0t = nc.dram_tensor("w0t", [D, D], F32, kind="ExternalInput")
    wqt = nc.dram_tensor("wqt", [D, 1024], F32, kind="ExternalInput")
    wot = nc.dram_tensor("wot", [D, D], F32, kind="ExternalInput")
    at_d = nc.dram_tensor("at", [D, 128], F32, kind="ExternalInput")
    bt_d = nc.dram_tensor("bt", [128, D], F32, kind="ExternalInput")
    m_d = nc.dram_tensor("m", [HKV, 128, 128], F32, kind="ExternalInput")
    zbd_d = nc.dram_tensor("zbd", [128, HKV * 8], F32, kind="ExternalInput")
    eh_d = nc.dram_tensor("eh", [HKV, HKV * 128], F32, kind="ExternalInput")
    b0c_d = nc.dram_tensor("b0c", [128, 32], F32, kind="ExternalInput")
    bqc_d = nc.dram_tensor("bqc", [128, 8], F32, kind="ExternalInput")
    m4_d = nc.dram_tensor("m4", [OT, 4, 128, TPC], BF16, kind="ExternalInput")
    out_d = nc.dram_tensor("out", [D, TPC], F32, kind="ExternalOutput")
    w0s = nc.dram_tensor("w0s", [OT, KT, 128, 128], BF16, kind="Internal")
    wqs = nc.dram_tensor("wqs", [HKV, KT, 128, 128], BF16, kind="Internal")
    wos = nc.dram_tensor("wos", [OT, HKV, 128, 128], BF16, kind="Internal")

    RS = nc.alloc_sbuf_tensor("RS", [128, KT * T], F32)
    XN = nc.alloc_sbuf_tensor("XN", [128, KT * T], BF16)
    EQ = nc.alloc_sbuf_tensor("EQ", [128, HKV * T], BF16)
    WS = [nc.alloc_sbuf_tensor(f"WS{i}", [128, KT * 128], BF16) for i in range(2)]
    MS = [nc.alloc_sbuf_tensor(f"MS{i}", [128, 4 * T], BF16) for i in range(2)]
    WF = [nc.alloc_sbuf_tensor(f"WF{i}", [128, 2048], F32) for i in range(2)]
    AT = nc.alloc_sbuf_tensor("AT", [128, KT * 128], BF16)
    BTs = nc.alloc_sbuf_tensor("BTs", [128, D], BF16)
    HB = nc.alloc_sbuf_tensor("HB", [128, T], BF16)
    X2 = [nc.alloc_sbuf_tensor(f"X2{i}", [128, T], BF16) for i in range(2)]
    RRb = nc.alloc_sbuf_tensor("RRb", [128, T], BF16)
    MSB = nc.alloc_sbuf_tensor("MSB", [128, HKV * 128], BF16)
    ZBD = nc.alloc_sbuf_tensor("ZBD", [128, HKV * 8], BF16)
    EH = nc.alloc_sbuf_tensor("EH", [HKV, HKV * 128], F32)
    DEN = nc.alloc_sbuf_tensor("DEN", [HKV, T], F32)
    RDEN = nc.alloc_sbuf_tensor("RDEN", [HKV, T], F32)
    STAT = nc.alloc_sbuf_tensor("STAT", [1, 4 * T], F32)
    RSTD = nc.alloc_sbuf_tensor("RSTD", [1, T], F32)
    ONE128B = nc.alloc_sbuf_tensor("ONE128B", [128, 1], BF16)
    ONEF32 = nc.alloc_sbuf_tensor("ONEF32", [1, 128], F32)
    B0H = nc.alloc_sbuf_tensor("B0H", [128, 32], F32)
    BQC = nc.alloc_sbuf_tensor("BQC", [128, 8], F32)
    BQN = nc.alloc_sbuf_tensor("BQN", [128, 8], F32)
    TACC = nc.alloc_sbuf_tensor("TACC", [128, T], BF16)
    TMUL = nc.alloc_sbuf_tensor("TMUL", [128, T], BF16)
    BPR = nc.alloc_sbuf_tensor("BPR", [128, T], BF16)

    P = [nc.alloc_psum_tensor(f"P{i}", [128, T], F32) for i in range(8)]

    sems = {n: nc.alloc_semaphore(n) for n in ("qS", "qG", "tP", "tA", "tV")}
    S = Sched(sems)
    AP = bass.AP

    pfree = {i: ("tV", 0) for i in range(8)}

    def pe_psum(i):
        sem, val = pfree[i]
        S.wait("tensor", sem, val)

    id_d = nc.dram_tensor("id128", [128, 128], F32, kind="ExternalInput")
    IDB = nc.alloc_sbuf_tensor("IDB", [128, 128], BF16)

    st = dict(
        last_rs_tv=[0] * KT,
        last_rs_qg=[0] * KT,
        xn_tv=[0] * KT,
        xn_read_tp=0,
        x2_tv=0,
        x2_pe=[0, 0],
        ws_pe=[0, 0],
        ws_wb_qg={},
        ws_wb_par=[0, 0],
        ms_free=[("tV", 0), ("tV", 0)],
        wf_ta=[0, 0],
        bpr_free_tv=0,
        eq_read_tp=0,
        eluq_wait=("tV", 0),
    )

    S.emit("vector", lambda e: e.memset(ONE128B[:, :], 1.0))
    S.emit("vector", lambda e: e.memset(ONEF32[:, :], 1.0))

    S.emit("gpsimd", lambda e: e.dma_start(out=B0H[:, :], in_=b0c_d[:]), sem="qG", by=16)
    S.emit("gpsimd", lambda e: e.dma_start(out=BQC[:, :], in_=bqc_d[:]), sem="qG", by=16)
    S.emit("gpsimd", lambda e: e.dma_start(out=EH[:, :], in_=eh_d[:]), sem="qG", by=16)
    g_zt = S.emit("gpsimd", lambda e: e.dma_start(out=WF[1][:, 0:HKV * 8], in_=zbd_d[:]),
                  sem="qG", by=16)
    ta = 0
    for hf in range(2):
        src = AP(at_d, hf * 16 * 128 * 128, [[128, 128], [128 * 128, 16], [1, 128]])
        dst = WF[0][:, 0:2048].rearrange("p (k f) -> p k f", f=128)
        S.wait("gpsimd", "tA", ta)
        qg = S.emit("gpsimd", lambda e, s=src, d=dst: e.dma_start(out=d, in_=s),
                    sem="qG", by=16)
        S.wait("scalar", "qG", qg)
        ta = S.emit("scalar", lambda e, hf=hf: e.copy(
            out=AT[:, hf * 2048:(hf + 1) * 2048], in_=WF[0][:, 0:2048]), sem="tA")
    S.wait("scalar", "qG", g_zt)
    ta = S.emit("scalar", lambda e: e.copy(
        out=ZBD[:, :], in_=WF[1][:, 0:HKV * 8]), sem="tA")
    for hf in range(2):
        src = AP(bt_d, hf * 2048, [[D, 128], [1, 2048]])
        S.wait("gpsimd", "tA", ta)
        qg = S.emit("gpsimd", lambda e, s=src: e.dma_start(out=WF[1][:, 0:2048], in_=s),
                    sem="qG", by=16)
        S.wait("scalar", "qG", qg)
        ta = S.emit("scalar", lambda e, hf=hf: e.copy(
            out=BTs[:, hf * 2048:(hf + 1) * 2048], in_=WF[1][:, 0:2048]), sem="tA")
    for h in range(HKV):
        src = AP(m_d, h * 128 * 128, [[128, 128], [1, 128]])
        S.wait("gpsimd", "tA", ta)
        qg = S.emit("gpsimd", lambda e, s=src: e.dma_start(out=WF[0][:, 0:128], in_=s),
                    sem="qG", by=16)
        S.wait("scalar", "qG", qg)
        ta = S.emit("scalar", lambda e, h=h: e.copy(
            out=MSB[:, h * 128:(h + 1) * 128], in_=WF[0][:, 0:128]), sem="tA")
    S.wait("gpsimd", "tA", ta)
    qg = S.emit("gpsimd", lambda e: e.dma_start(out=WF[0][:, 0:128], in_=id_d[:]),
                sem="qG", by=16)
    S.wait("scalar", "qG", qg)
    ta = S.emit("scalar", lambda e: e.copy(out=IDB[:, :], in_=WF[0][:, 0:128]), sem="tA")
    st["wf_ta"] = [ta, ta]
    S.wait("vector", "qG", g_zt)
    S.wait("tensor", "qG", g_zt)
    S.emit("vector", lambda e: e.tensor_scalar_mul(B0H[:, :], B0H[:, :], 0.5))
    setup_tv = S.emit("vector", lambda e: e.tensor_scalar_mul(
        BQN[:, :], BQC[:, :], -1.0), sem="tV")
    S.wait("tensor", "tV", setup_tv)
    S.wait("scalar", "tV", setup_tv)
    S.wait("sync", "tV", setup_tv)
    S.wait("gpsimd", "tV", setup_tv)

    def ln_phase(chunk, phase):
        pe_psum(0)
        pe_psum(1)
        stat_tp = 0
        S.wait("scalar", *st["eluq_wait"])
        for k in range(KT):
            if phase == 0:
                S.wait("scalar", "qG", st["last_rs_qg"][KT - 1])
            else:
                S.wait("scalar", "tV", st["last_rs_tv"][k])
            if k == 0:
                S.wait("scalar", "tP", st["xn_read_tp"])
            S.emit("scalar", lambda e, k=k: e.copy(
                out=XN[:, k * T:(k + 1) * T], in_=RS[:, k * T:(k + 1) * T]))
            S.wait("scalar", "tP", st["x2_pe"][k % 2])
            ta = S.emit("scalar", lambda e, k=k: e.activation(
                out=X2[k % 2][:, :], in_=XN[:, k * T:(k + 1) * T], func=AF.Square),
                sem="tA")
            S.wait("tensor", "tA", ta)
            S.emit("tensor", lambda e, k=k: e.matmul(
                P[0][0:1, :], ONE128B[:, :], XN[:, k * T:(k + 1) * T],
                start=(k == 0), stop=(k == KT - 1)))
            stat_tp = S.emit("tensor", lambda e, k=k: e.matmul(
                P[1][0:1, :], ONE128B[:, :], X2[k % 2][:, :],
                start=(k == 0), stop=(k == KT - 1)), sem="tP")
            st["x2_pe"][k % 2] = stat_tp
        S.wait("vector", "tP", stat_tp)
        MU, EX2, VAR, RVAR = (STAT[0:1, i * T:(i + 1) * T] for i in range(4))
        S.emit("vector", lambda e: e.tensor_scalar_mul(MU, P[0][0:1, :], 1.0 / D))
        S.emit("vector", lambda e: e.tensor_scalar_mul(EX2, P[1][0:1, :], 1.0 / D))
        S.emit("vector", lambda e: e.tensor_mul(VAR, MU, MU))
        S.emit("vector", lambda e: e.tensor_sub(VAR, EX2, VAR))
        S.emit("vector", lambda e: e.tensor_scalar_add(VAR, VAR, 1e-5))
        v1 = S.emit("vector", lambda e: e.reciprocal(RVAR, VAR), sem="tV")
        pfree[0] = ("tV", v1)
        pfree[1] = ("tV", v1)
        S.wait("scalar", "tV", v1)
        a1 = S.emit("scalar", lambda e: e.activation(
            out=RSTD[0:1, :], in_=RVAR, func=AF.Sqrt), sem="tA")
        S.wait("tensor", "tA", a1)
        pe_psum(2)
        pe_psum(3)
        S.emit("tensor", lambda e: e.matmul(
            P[2][:, :], ONEF32[:, :], MU, start=True, stop=True))
        rep_tp = S.emit("tensor", lambda e: e.matmul(
            P[3][:, :], ONEF32[:, :], RSTD[0:1, :], start=True, stop=True), sem="tP")
        S.wait("vector", "tP", rep_tp)
        S.wait("vector", "tA", ta)
        for k in range(KT):
            S.emit("vector", lambda e, k=k: e.tensor_sub(
                X2[k % 2][:, :], XN[:, k * T:(k + 1) * T], P[2][:, :]))
            st["xn_tv"][k] = S.emit("vector", lambda e, k=k: e.tensor_mul(
                XN[:, k * T:(k + 1) * T], X2[k % 2][:, :], P[3][:, :]), sem="tV")
        st["x2_tv"] = st["xn_tv"][KT - 1]
        st["eluq_wait"] = ("tV", st["x2_tv"])
        pfree[2] = ("tV", st["xn_tv"][KT - 1])
        pfree[3] = ("tV", st["xn_tv"][KT - 1])

    def stream_strip(kind, first, idx, par, nkt):
        scratch = {"w0": w0s, "wq": wqs, "wo": wos}[kind]
        if first:
            if kind == "wo":
                for hf in range(2):
                    src = AP(wot, (hf * 16 * 128) * D + idx * 128,
                             [[D, 128], [128 * D, 16], [1, 128]])
                    dst = WF[hf][:, 0:2048].rearrange("p (k f) -> p k f", f=128)
                    S.wait("sync", "tA", st["wf_ta"][hf])
                    qs = S.emit("sync", lambda e, s=src, d=dst: e.dma_start(
                        out=d, in_=s), sem="qS", by=16)
                    S.wait("scalar", "qS", qs)
                    S.wait("scalar", *st["ms_free"][hf])
                    ta = S.emit("scalar", lambda e, hf=hf: e.copy(
                        out=MS[hf][:, 0:2048], in_=WF[hf][:, 0:2048]), sem="tA")
                    st["wf_ta"][hf] = ta
                S.wait("tensor", "tA", ta)
                pe_psum(5)
                pe_psum(6)
                fold_tp = 0
                for j in range(HKV):
                    for g in range(4):
                        b = 4 * j + g
                        fold_tp = S.emit("tensor", lambda e, j=j, g=g, b=b: e.matmul(
                            P[5 + j // 4][:, (j % 4) * 128:(j % 4 + 1) * 128],
                            IDB[:, :],
                            MS[b // 16][:, (b % 16) * 128:(b % 16 + 1) * 128],
                            start=(g == 0), stop=(g == 3)),
                            sem=("tP" if (j == HKV - 1 and g == 3) else None))
                st["ms_free"][0] = ("tP", fold_tp)
                st["ms_free"][1] = ("tP", fold_tp)
                S.wait("scalar", "tP", fold_tp)
                S.wait("scalar", "tP", st["ws_pe"][par])
                S.wait("scalar", "qG", st["ws_wb_par"][par])
                S.emit("scalar", lambda e, par=par: e.copy(
                    out=WS[par][:, 0:512], in_=P[5][:, :]))
                ta = S.emit("scalar", lambda e, par=par: e.copy(
                    out=WS[par][:, 512:1024], in_=P[6][:, :]), sem="tA")
                pfree[5] = ("tA", ta)
                pfree[6] = ("tA", ta)
            else:
                src_t, width = (w0t, D) if kind == "w0" else (wqt, 1024)
                ta = 0
                for hf in range(2):
                    src = AP(src_t, (hf * 16 * 128) * width + idx * 128,
                             [[width, 128], [128 * width, 16], [1, 128]])
                    dst = WF[hf][:, 0:2048].rearrange("p (k f) -> p k f", f=128)
                    S.wait("sync", "tA", st["wf_ta"][hf])
                    qs = S.emit("sync", lambda e, s=src, d=dst: e.dma_start(
                        out=d, in_=s), sem="qS", by=16)
                    S.wait("scalar", "qS", qs)
                    if hf == 0:
                        S.wait("scalar", "tP", st["ws_pe"][par])
                        S.wait("scalar", "qG", st["ws_wb_par"][par])
                    ta = S.emit("scalar", lambda e, hf=hf, par=par: e.copy(
                        out=WS[par][:, hf * 2048:(hf + 1) * 2048],
                        in_=WF[hf][:, 0:2048]), sem="tA")
                    st["wf_ta"][hf] = ta
            S.wait("gpsimd", "tA", ta)
            dstw = AP(scratch, idx * nkt * 128 * 128,
                      [[128, 128], [128 * 128, nkt], [1, 128]])
            srcw = WS[par][:, 0:nkt * 128].rearrange("p (k f) -> p k f", f=128)
            wb = S.emit("gpsimd", lambda e, d=dstw, s=srcw: e.dma_start(out=d, in_=s),
                        sem="qG", by=16)
            st["ws_wb_qg"][(kind, idx)] = wb
            st["ws_wb_par"][par] = wb
            return ("tA", ta)
        else:
            S.wait("sync", "tP", st["ws_pe"][par])
            S.wait("sync", "qG", max(st["ws_wb_qg"].get((kind, idx), 0),
                                     st["ws_wb_par"][par]))
            src = AP(scratch, idx * nkt * 128 * 128,
                     [[128, 128], [128 * 128, nkt], [1, 128]])
            dst = WS[par][:, 0:nkt * 128].rearrange("p (k f) -> p k f", f=128)
            qs = S.emit("sync", lambda e, s=src, d=dst: e.dma_start(out=d, in_=s),
                        sem="qS", by=16)
            return ("qS", qs)

    def ffn_phase(chunk, call):
        pe_psum(5)
        h_tp = 0
        for k in range(KT):
            S.wait("tensor", "tV", st["xn_tv"][k])
            h_tp = S.emit("tensor", lambda e, k=k: e.matmul(
                P[5][:, :], AT[:, k * 128:(k + 1) * 128], XN[:, k * T:(k + 1) * T],
                start=(k == 0), stop=(k == KT - 1)), sem="tP")
        S.wait("scalar", "tP", h_tp)
        hb_ta = S.emit("scalar", lambda e: e.copy(out=HB[:, :], in_=P[5][:, :]), sem="tA")
        pfree[5] = ("tA", hb_ta)

        firstpass = (chunk == 0 and call == 1)
        for o in range(OT):
            par = o % 2
            sem, val = stream_strip("w0", firstpass, o, par, KT)
            S.wait("sync", *st["ms_free"][par])
            msrc = AP(m4_d, (o * 4 * 128) * TPC + chunk * T,
                      [[TPC, 128], [128 * TPC, 4], [1, T]])
            mdst = MS[par][:, :].rearrange("p (l t) -> p l t", t=T)
            mq = S.emit("sync", lambda e, s=msrc, d=mdst: e.dma_start(out=d, in_=s),
                        sem="qS", by=16)
            S.wait("tensor", sem, val)
            pe_psum(0)
            for i in range(1, 5):
                pe_psum(i)
            S.wait("tensor", "tA", hb_ta)
            for k in range(KT):
                S.emit("tensor", lambda e, k=k, par=par: e.matmul(
                    P[0][:, :], WS[par][:, k * 128:(k + 1) * 128],
                    XN[:, k * T:(k + 1) * T],
                    start=(k == 0), stop=(k == KT - 1)))
            pe_o = 0
            for l in range(4):
                pe_o = S.emit("tensor", lambda e, l=l, o=o: e.matmul(
                    P[1 + l][:, :],
                    BTs[32 * l:32 * (l + 1), o * 128:(o + 1) * 128],
                    HB[32 * l:32 * (l + 1), :],
                    start=True, stop=True, tile_position=(32 * l, 0)),
                    sem=("tP" if l == 3 else None))
            st["ws_pe"][par] = pe_o
            if o == OT - 1:
                st["xn_read_tp"] = pe_o
            S.wait("scalar", "tP", pe_o)
            S.wait("scalar", "tV", st["bpr_free_tv"])
            bpr_ta = S.emit("scalar", lambda e, o=o: e.activation(
                out=BPR[:, :], in_=P[0][:, :], func=AF.Identity,
                bias=B0H[:, o:o + 1], scale=0.5), sem="tA")
            pfree[0] = ("tA", bpr_ta)
            S.wait("vector", "tP", pe_o)
            S.wait("vector", "qS", mq)
            S.wait("vector", "tA", bpr_ta)
            S.emit("vector", lambda e, par=par: e.tensor_mul(
                TACC[:, :], P[1][:, :], MS[par][:, 0:T]))
            for l in range(1, 4):
                S.emit("vector", lambda e, l=l, par=par: e.tensor_mul(
                    TMUL[:, :], P[1 + l][:, :], MS[par][:, l * T:(l + 1) * T]))
                S.emit("vector", lambda e: e.tensor_add(TACC[:, :], TACC[:, :], TMUL[:, :]))
            S.emit("vector", lambda e: e.tensor_add(TACC[:, :], TACC[:, :], BPR[:, :]))
            tv = S.emit("vector", lambda e, o=o: e.tensor_add(
                RS[:, o * T:(o + 1) * T], RS[:, o * T:(o + 1) * T], TACC[:, :]),
                sem="tV")
            for i in range(1, 5):
                pfree[i] = ("tV", tv)
            st["ms_free"][par] = ("tV", tv)
            st["bpr_free_tv"] = tv
            st["last_rs_tv"][o] = tv
            if call == 2:
                S.wait("gpsimd", "tV", tv)
                dsto = AP(out_d, (o * 128) * TPC + chunk * T, [[TPC, 128], [1, T]])
                S.emit("gpsimd", lambda e, d=dsto, o=o: e.dma_start(
                    out=d, in_=RS[:, o * T:(o + 1) * T]), sem="qG", by=16)

    def attn_phase(chunk):
        first = (chunk == 0)
        eluq_tv = 0
        for hh in range(HKV):
            par = hh % 2
            sem, val = stream_strip("wq", first, hh, par, KT)
            S.wait("tensor", sem, val)
            pe_psum(0)
            q_tp = 0
            for k in range(KT):
                S.wait("tensor", "tV", st["xn_tv"][k])
                q_tp = S.emit("tensor", lambda e, k=k, par=par: e.matmul(
                    P[0][:, :], WS[par][:, k * 128:(k + 1) * 128],
                    XN[:, k * T:(k + 1) * T],
                    start=(k == 0), stop=(k == KT - 1)),
                    sem=("tP" if k == KT - 1 else None))
            st["ws_pe"][par] = q_tp
            if hh == HKV - 1:
                st["xn_read_tp"] = q_tp
            S.wait("scalar", "tP", q_tp)
            S.wait("scalar", *st["eluq_wait"])
            S.emit("scalar", lambda e, hh=hh: e.activation(
                out=X2[0][:, :], in_=P[0][:, :], func=AF.Relu,
                bias=BQN[:, hh:hh + 1], scale=-1.0))
            S.emit("scalar", lambda e: e.activation(
                out=X2[1][:, :], in_=X2[0][:, :], func=AF.Exp, scale=-1.0))
            elu_ta = S.emit("scalar", lambda e, hh=hh: e.activation(
                out=X2[0][:, :], in_=P[0][:, :], func=AF.Relu,
                bias=BQC[:, hh:hh + 1]), sem="tA")
            pfree[0] = ("tA", elu_ta)
            S.wait("vector", "tA", elu_ta)
            if hh == 0:
                S.wait("vector", "tP", st["eq_read_tp"])
            eluq_tv = S.emit("vector", lambda e, hh=hh: e.tensor_add(
                EQ[:, hh * T:(hh + 1) * T], X2[1][:, :], X2[0][:, :]), sem="tV")
            st["eluq_wait"] = ("tV", eluq_tv)
        st["x2_tv"] = eluq_tv
        S.wait("tensor", "tV", eluq_tv)
        pe_psum(3)
        den_tp = 0
        for h in range(HKV):
            den_tp = S.emit("tensor", lambda e, h=h: e.matmul(
                P[3][0:HKV, :], ZBD[:, h * 8:(h + 1) * 8], EQ[:, h * T:(h + 1) * T],
                start=(h == 0), stop=(h == HKV - 1)),
                sem=("tP" if h == HKV - 1 else None))
        S.wait("vector", "tP", den_tp)
        S.emit("vector", lambda e: e.tensor_scalar(
            DEN[:, :], P[3][0:HKV, :], 1e-6, None, op0=ALU.add))
        rden_tv = S.emit("vector", lambda e: e.reciprocal(RDEN[:, :], DEN[:, :]), sem="tV")
        pfree[3] = ("tV", rden_tv)
        retr_tv = 0
        for h in range(HKV):
            pe_psum(1)
            pe_psum(2)
            S.emit("tensor", lambda e, h=h: e.matmul(
                P[1][:, :], MSB[:, h * 128:(h + 1) * 128], EQ[:, h * T:(h + 1) * T],
                start=True, stop=True))
            S.wait("tensor", "tV", rden_tv)
            num_tp = S.emit("tensor", lambda e, h=h: e.matmul(
                P[2][:, :], EH[:, h * 128:(h + 1) * 128], RDEN[:, :],
                start=True, stop=True), sem="tP")
            S.wait("scalar", "tP", num_tp)
            rrep_ta = S.emit("scalar", lambda e: e.copy(out=RRb[:, :], in_=P[2][:, :]),
                             sem="tA")
            pfree[2] = ("tA", rrep_ta)
            S.wait("vector", "tP", num_tp)
            S.wait("vector", "tA", rrep_ta)
            retr_tv = S.emit("vector", lambda e, h=h: e.tensor_mul(
                EQ[:, h * T:(h + 1) * T], P[1][:, :], RRb[:, :]), sem="tV")
            pfree[1] = ("tV", retr_tv)
        S.wait("tensor", "tV", retr_tv)
        wo_tp = 0
        for o in range(OT):
            par = o % 2
            sem, val = stream_strip("wo", first, o, par, HKV)
            S.wait("tensor", sem, val)
            pe_psum(4)
            for k in range(HKV):
                wo_tp = S.emit("tensor", lambda e, k=k, par=par: e.matmul(
                    P[4][:, :], WS[par][:, k * 128:(k + 1) * 128],
                    EQ[:, k * T:(k + 1) * T],
                    start=(k == 0), stop=(k == HKV - 1)),
                    sem=("tP" if k == HKV - 1 else None))
            st["ws_pe"][par] = wo_tp
            S.wait("vector", "tP", wo_tp)
            tv = S.emit("vector", lambda e, o=o: e.tensor_add(
                RS[:, o * T:(o + 1) * T], RS[:, o * T:(o + 1) * T], P[4][:, :]),
                sem="tV")
            pfree[4] = ("tV", tv)
            st["last_rs_tv"][o] = tv
        st["eq_read_tp"] = wo_tp

    stop = globals().get("DEBUG_STOP")
    nch = globals().get("DEBUG_NCH", NCH)
    done = False
    for c in range(nch):
        if done:
            break
        for k in range(KT):
            src = AP(xt, (k * 128) * TPC + c * T, [[TPC, 128], [1, T]])
            st["last_rs_qg"][k] = S.emit("gpsimd", lambda e, s=src, k=k: e.dma_start(
                out=RS[:, k * T:(k + 1) * T], in_=s), sem="qG", by=16,
                fence=(k == KT - 1))
        for name, fn in (("ln1", lambda: ln_phase(c, 0)),
                         ("ffn1", lambda: ffn_phase(c, 1)),
                         ("ln2", lambda: ln_phase(c, 1)),
                         ("attn", lambda: attn_phase(c)),
                         ("ln3", lambda: ln_phase(c, 2)),
                         ("ffn2", lambda: ffn_phase(c, 2))):
            fn()
            if stop == name:
                done = True
                break

    if stop is not None:
        S.wait("gpsimd", "tP", S.cnt["tP"])
        S.wait("gpsimd", "tA", S.cnt["tA"])
        S.wait("gpsimd", "tV", S.cnt["tV"])
        for k in range(KT):
            dsto = AP(out_d, (k * 128) * TPC, [[TPC, 128], [1, T]])
            S.emit("gpsimd", lambda e, d=dsto, k=k: e.dma_start(
                out=d, in_=RS[:, k * T:(k + 1) * T]), sem="qG", by=16,
                fence=(k == KT - 1))
        for k in range(KT):
            dsto = AP(out_d, (k * 128) * TPC + T, [[TPC, 128], [1, T // 2]])
            S.emit("gpsimd", lambda e, d=dsto, k=k: e.dma_start(
                out=d, in_=XN[:, k * T:(k + 1) * T].bitcast(F32)), sem="qG", by=16,
                fence=(k == KT - 1))
        S.emit("gpsimd", lambda e: e.dma_start(
            out=AP(out_d, 300 * TPC, [[TPC, 1], [1, T]]), in_=STAT[0:1, 0:T]), sem="qG", by=16)
        S.emit("gpsimd", lambda e: e.dma_start(
            out=AP(out_d, 301 * TPC, [[TPC, 1], [1, T]]), in_=RSTD[0:1, :]), sem="qG", by=16)
        S.emit("gpsimd", lambda e: e.dma_start(
            out=AP(out_d, 302 * TPC, [[TPC, 1], [1, T]]),
            in_=STAT[0:1, 3 * T:4 * T]), sem="qG", by=16)

    S.wait("gpsimd", "qG", S.cnt["qG"])

    with nc.Block() as block:
        @block.tensor
        def _(e):
            for f in S.prog["tensor"]:
                f(e)

        @block.scalar
        def _(e):
            for f in S.prog["scalar"]:
                f(e)

        @block.vector
        def _(e):
            for f in S.prog["vector"]:
                f(e)

        @block.sync
        def _(e):
            for f in S.prog["sync"]:
                f(e)

        @block.gpsimd
        def _(e):
            for f in S.prog["gpsimd"]:
                f(e)

    return nc


def kernel(L0_attn_hidden_states, L0_attn_Wq, L0_attn_bq, L0_attn_Wk, L0_attn_bk,
           L0_attn_Wv, L0_attn_bv, L0_attn_Wo, L0_attn_M, L0_attn_z,
           L0_ffn_W0, L0_ffn_b0, L0_ffn_A, L0_ffn_B):
    x = np.asarray(L0_attn_hidden_states, dtype=np.float32)
    Wq = np.asarray(L0_attn_Wq, dtype=np.float32)
    bq = np.asarray(L0_attn_bq, dtype=np.float32)
    Wo = np.asarray(L0_attn_Wo, dtype=np.float32)
    M = np.asarray(L0_attn_M, dtype=np.float32)
    z = np.asarray(L0_attn_z, dtype=np.float32)
    W0 = np.asarray(L0_ffn_W0, dtype=np.float32)
    b0 = np.asarray(L0_ffn_b0, dtype=np.float32)
    A = np.asarray(L0_ffn_A, dtype=np.float32)
    B = np.asarray(L0_ffn_B, dtype=np.float32)

    xt_full = np.ascontiguousarray(x.reshape(16384, D).T)
    w0t = np.ascontiguousarray(W0.T)
    sel_rows = np.concatenate([np.arange(h * 128, (h + 1) * 128)
                               for h in range(0, 32, 4)])
    wqt = np.ascontiguousarray(Wq[sel_rows].T)
    bq_sel = bq[sel_rows]
    wot = np.ascontiguousarray(Wo.T)
    atp = np.zeros((128, D), np.float32)
    for l in range(4):
        atp[32 * l:32 * l + 16] = A[l]
    at_h = np.ascontiguousarray(atp.T)
    btp = np.zeros((128, D), np.float32)
    for l in range(4):
        btp[32 * l:32 * l + 16] = B[l].T
    bt_h = np.ascontiguousarray(btp)
    zbd_h = np.zeros((128, HKV * 8), np.float32)
    for h in range(HKV):
        zbd_h[:, h * 8 + h] = z[h]
    eh_h = np.zeros((HKV, HKV * 128), np.float32)
    for h in range(HKV):
        eh_h[h, h * 128:(h + 1) * 128] = 1.0
    b0c = np.ascontiguousarray(b0.reshape(32, 128).T)
    bqc = np.ascontiguousarray(bq_sel.reshape(8, 128).T)

    sel = _mask_sel()

    nc = build_program()

    in_maps = []
    for c in range(N_CORES):
        t0, t1 = c * TPC, (c + 1) * TPC
        xt_c = np.ascontiguousarray(xt_full[:, t0:t1])
        blk = sel[t0:t1, :]
        m4 = np.empty((OT, 4, 128, TPC), dtype=ml_dtypes.bfloat16)
        for l in range(4):
            arr = (blk == l).T.reshape(OT, 128, TPC)
            m4[:, l] = (arr.astype(np.float32) * 4.0).astype(ml_dtypes.bfloat16)
        in_maps.append({
            "xt": xt_c, "w0t": w0t, "wqt": wqt, "wot": wot,
            "at": at_h, "bt": bt_h, "m": M, "zbd": zbd_h, "eh": eh_h,
            "id128": np.eye(128, dtype=np.float32),
            "b0c": b0c, "bqc": bqc, "m4": m4,
        })

    res = run_bass_kernel_spmd(nc, in_maps, core_ids=list(range(N_CORES)))
    outs = [np.asarray(r["out"]) for r in res.results]
    full = np.concatenate(outs, axis=1)
    return np.ascontiguousarray(full.T).reshape(4, 4096, 4096)


# version 05
# speedup vs baseline: 1.0191x; 1.0191x over previous
import os
import numpy as np
import ml_dtypes

import concourse.bass as bass
import concourse.mybir as mybir
from concourse.bass_utils import run_bass_kernel_spmd

F32 = mybir.dt.float32
BF16 = mybir.dt.bfloat16
AF = mybir.ActivationFunctionType
ALU = mybir.AluOpType

N_CORES = 8
D = 4096
KT = 32
TPC = 2048
NCH = 4
T = 512
OT = 32
HKV = 8
ENGS = ("tensor", "scalar", "vector", "sync", "gpsimd")


def _mask_sel():
    import jax
    import jax.numpy as jnp

    use64 = bool(os.environ.get("BASS_SIM_FP64"))
    if use64:
        jax.config.update("jax_enable_x64", True)
    cpu = jax.devices("cpu")[0]
    with jax.default_device(cpu):
        key = jax.random.key(42, impl="threefry2x32")
        dt = jnp.float64 if use64 else jnp.float32
        u = np.asarray(jax.random.uniform(key, (4, 4096, 4096), dtype=dt))
    sel = (u * 4).astype(np.uint8)
    return sel.reshape(16384, 4096)


class Sched:

    def __init__(self, sems):
        self.sems = sems
        self.prog = {e: [] for e in ENGS}
        self.cnt = {k: 0 for k in sems}
        self.last_wait = {e: {k: 0 for k in sems} for e in ENGS}

    def wait(self, eng, sem, val):
        if val <= 0:
            return
        assert val <= self.cnt[sem], f"wait {sem}>={val} > produced {self.cnt[sem]}"
        if self.last_wait[eng][sem] >= val:
            return
        self.last_wait[eng][sem] = val
        h = self.sems[sem]
        self.prog[eng].append(lambda e, h=h, v=val: e.wait_ge(h, v))

    def inc(self, sem, by):
        self.cnt[sem] += by
        return self.cnt[sem]

    def emit(self, eng, make, sem=None, by=1, fence=None):
        if eng in ("vector", "scalar"):
            self.prog[eng].append(lambda e: e.drain())
        if sem is None:
            self.prog[eng].append(make)
            return None
        n = self.inc(sem, by)
        h = self.sems[sem]
        self.prog[eng].append(lambda e, h=h, b=by: make(e).then_inc(h, b))
        if fence is None:
            fence = sem in ("qS", "qG")
        if fence and sem in ("qS", "qG"):
            self.wait(eng, sem, n)
        return n


def build_program():
    nc = bass.Bass("TRN2", target_bir_lowering=False, debug=False)

    xt = nc.dram_tensor("xt", [D, TPC], F32, kind="ExternalInput")
    w0t = nc.dram_tensor("w0t", [D, D], F32, kind="ExternalInput")
    wqt = nc.dram_tensor("wqt", [D, 1024], F32, kind="ExternalInput")
    wot = nc.dram_tensor("wot", [D, D], F32, kind="ExternalInput")
    at_d = nc.dram_tensor("at", [D, 128], F32, kind="ExternalInput")
    bt_d = nc.dram_tensor("bt", [128, D], F32, kind="ExternalInput")
    m_d = nc.dram_tensor("m", [HKV, 128, 128], F32, kind="ExternalInput")
    zbd_d = nc.dram_tensor("zbd", [128, HKV * 8], F32, kind="ExternalInput")
    eh_d = nc.dram_tensor("eh", [HKV, HKV * 128], F32, kind="ExternalInput")
    b0c_d = nc.dram_tensor("b0c", [128, 32], F32, kind="ExternalInput")
    bqc_d = nc.dram_tensor("bqc", [128, 8], F32, kind="ExternalInput")
    m4_d = nc.dram_tensor("m4", [OT, 4, 128, TPC], BF16, kind="ExternalInput")
    out_d = nc.dram_tensor("out", [D, TPC], F32, kind="ExternalOutput")
    w0s = nc.dram_tensor("w0s", [OT, KT, 128, 128], BF16, kind="Internal")
    wqs = nc.dram_tensor("wqs", [HKV, KT, 128, 128], BF16, kind="Internal")
    wos = nc.dram_tensor("wos", [OT, HKV, 128, 128], BF16, kind="Internal")

    RS = nc.alloc_sbuf_tensor("RS", [128, KT * T], F32)
    XN = nc.alloc_sbuf_tensor("XN", [128, KT * T], BF16)
    WS = [nc.alloc_sbuf_tensor(f"WS{i}", [128, KT * 128], BF16) for i in range(2)]
    MS = [nc.alloc_sbuf_tensor(f"MS{i}", [128, 4 * T], BF16) for i in range(2)]
    WF = [nc.alloc_sbuf_tensor(f"WF{i}", [128, 2048], F32) for i in range(2)]
    AT = nc.alloc_sbuf_tensor("AT", [128, KT * 128], BF16)
    BTs = nc.alloc_sbuf_tensor("BTs", [128, D], BF16)
    HB = nc.alloc_sbuf_tensor("HB", [128, T], BF16)
    X2G = [nc.alloc_sbuf_tensor(f"X2G{i}", [128, 8 * T], BF16) for i in range(2)]
    X2 = [X2G[0][:, 0:T], X2G[0][:, T:2 * T]]
    EQ = X2G[1]
    YS = [nc.alloc_sbuf_tensor(f"YS{i}", [128, 4 * T], BF16) for i in range(2)]
    BS = [nc.alloc_sbuf_tensor(f"BS{i}", [128, T], BF16) for i in range(2)]
    RRb = nc.alloc_sbuf_tensor("RRb", [128, T], BF16)
    MSB = nc.alloc_sbuf_tensor("MSB", [128, HKV * 128], BF16)
    ZBD = nc.alloc_sbuf_tensor("ZBD", [128, HKV * 8], BF16)
    EH = nc.alloc_sbuf_tensor("EH", [HKV, HKV * 128], F32)
    DEN = nc.alloc_sbuf_tensor("DEN", [HKV, T], F32)
    RDEN = nc.alloc_sbuf_tensor("RDEN", [HKV, T], F32)
    STAT = nc.alloc_sbuf_tensor("STAT", [1, 4 * T], F32)
    RSTD = nc.alloc_sbuf_tensor("RSTD", [1, T], F32)
    ONE128B = nc.alloc_sbuf_tensor("ONE128B", [128, 1], BF16)
    ONEF32 = nc.alloc_sbuf_tensor("ONEF32", [1, 128], F32)
    B0H = nc.alloc_sbuf_tensor("B0H", [128, 32], F32)
    BQC = nc.alloc_sbuf_tensor("BQC", [128, 8], F32)
    BQN = nc.alloc_sbuf_tensor("BQN", [128, 8], F32)
    TACC = nc.alloc_sbuf_tensor("TACC", [128, T], BF16)
    TMUL = nc.alloc_sbuf_tensor("TMUL", [128, T], BF16)

    P = [nc.alloc_psum_tensor(f"P{i}", [128, T], F32) for i in range(8)]

    sems = {n: nc.alloc_semaphore(n) for n in ("qS", "qG", "tP", "tA", "tV")}
    S = Sched(sems)
    AP = bass.AP

    pfree = {i: ("tV", 0) for i in range(8)}

    def pe_psum(i):
        sem, val = pfree[i]
        S.wait("tensor", sem, val)

    id_d = nc.dram_tensor("id128", [128, 128], F32, kind="ExternalInput")
    IDB = nc.alloc_sbuf_tensor("IDB", [128, 128], BF16)

    st = dict(
        last_rs_tv=[0] * KT,
        last_rs_qg=[0] * KT,
        xn_tv=[0] * KT,
        xn_read_tp=0,
        x2_tv=0,
        x2_pe=[0, 0],
        ws_pe=[0, 0],
        ws_wb_qg={},
        ws_wb_par=[0, 0],
        ms_free=[("tV", 0), ("tV", 0)],
        wf_ta=[0, 0],
        bpr_free_tv=[0, 0],
        eq_read_tp=0,
        eluq_wait=("tV", 0),
    )

    S.emit("vector", lambda e: e.memset(ONE128B[:, :], 1.0))
    S.emit("vector", lambda e: e.memset(ONEF32[:, :], 1.0))

    S.emit("gpsimd", lambda e: e.dma_start(out=B0H[:, :], in_=b0c_d[:]), sem="qG", by=16)
    S.emit("gpsimd", lambda e: e.dma_start(out=BQC[:, :], in_=bqc_d[:]), sem="qG", by=16)
    S.emit("gpsimd", lambda e: e.dma_start(out=EH[:, :], in_=eh_d[:]), sem="qG", by=16)
    g_zt = S.emit("gpsimd", lambda e: e.dma_start(out=WF[1][:, 0:HKV * 8], in_=zbd_d[:]),
                  sem="qG", by=16)
    ta = 0
    for hf in range(2):
        src = AP(at_d, hf * 16 * 128 * 128, [[128, 128], [128 * 128, 16], [1, 128]])
        dst = WF[0][:, 0:2048].rearrange("p (k f) -> p k f", f=128)
        S.wait("gpsimd", "tA", ta)
        qg = S.emit("gpsimd", lambda e, s=src, d=dst: e.dma_start(out=d, in_=s),
                    sem="qG", by=16)
        S.wait("scalar", "qG", qg)
        ta = S.emit("scalar", lambda e, hf=hf: e.copy(
            out=AT[:, hf * 2048:(hf + 1) * 2048], in_=WF[0][:, 0:2048]), sem="tA")
    S.wait("scalar", "qG", g_zt)
    ta = S.emit("scalar", lambda e: e.copy(
        out=ZBD[:, :], in_=WF[1][:, 0:HKV * 8]), sem="tA")
    for hf in range(2):
        src = AP(bt_d, hf * 2048, [[D, 128], [1, 2048]])
        S.wait("gpsimd", "tA", ta)
        qg = S.emit("gpsimd", lambda e, s=src: e.dma_start(out=WF[1][:, 0:2048], in_=s),
                    sem="qG", by=16)
        S.wait("scalar", "qG", qg)
        ta = S.emit("scalar", lambda e, hf=hf: e.copy(
            out=BTs[:, hf * 2048:(hf + 1) * 2048], in_=WF[1][:, 0:2048]), sem="tA")
    for h in range(HKV):
        src = AP(m_d, h * 128 * 128, [[128, 128], [1, 128]])
        S.wait("gpsimd", "tA", ta)
        qg = S.emit("gpsimd", lambda e, s=src: e.dma_start(out=WF[0][:, 0:128], in_=s),
                    sem="qG", by=16)
        S.wait("scalar", "qG", qg)
        ta = S.emit("scalar", lambda e, h=h: e.copy(
            out=MSB[:, h * 128:(h + 1) * 128], in_=WF[0][:, 0:128]), sem="tA")
    S.wait("gpsimd", "tA", ta)
    qg = S.emit("gpsimd", lambda e: e.dma_start(out=WF[0][:, 0:128], in_=id_d[:]),
                sem="qG", by=16)
    S.wait("scalar", "qG", qg)
    ta = S.emit("scalar", lambda e: e.copy(out=IDB[:, :], in_=WF[0][:, 0:128]), sem="tA")
    st["wf_ta"] = [ta, ta]
    S.wait("vector", "qG", g_zt)
    S.wait("tensor", "qG", g_zt)
    S.emit("vector", lambda e: e.tensor_scalar_mul(B0H[:, :], B0H[:, :], 0.5))
    setup_tv = S.emit("vector", lambda e: e.tensor_scalar_mul(
        BQN[:, :], BQC[:, :], -1.0), sem="tV")
    S.wait("tensor", "tV", setup_tv)
    S.wait("scalar", "tV", setup_tv)
    S.wait("sync", "tV", setup_tv)
    S.wait("gpsimd", "tV", setup_tv)

    def ln_phase(chunk, phase):
        G = 8
        NG = KT // G
        pe_psum(0)
        pe_psum(1)
        stat_tp = 0
        S.wait("scalar", *st["eluq_wait"])
        S.wait("scalar", "tP", st["xn_read_tp"])
        for g in range(NG):
            lo, hi = g * G * T, (g + 1) * G * T
            if phase == 0:
                S.wait("scalar", "qG", st["last_rs_qg"][KT - 1])
            else:
                for k in range(g * G, (g + 1) * G):
                    S.wait("scalar", "tV", st["last_rs_tv"][k])
            S.emit("scalar", lambda e, lo=lo, hi=hi: e.copy(
                out=XN[:, lo:hi], in_=RS[:, lo:hi]))
            S.wait("scalar", "tP", st["x2_pe"][g % 2])
            ta = S.emit("scalar", lambda e, g=g, lo=lo, hi=hi: e.activation(
                out=X2G[g % 2][:, :], in_=XN[:, lo:hi], func=AF.Square), sem="tA")
            S.wait("tensor", "tA", ta)
            for k in range(G):
                kk = g * G + k
                S.emit("tensor", lambda e, kk=kk: e.matmul(
                    P[0][0:1, :], ONE128B[:, :], XN[:, kk * T:(kk + 1) * T],
                    start=(kk == 0), stop=(kk == KT - 1)))
                stat_tp = S.emit("tensor", lambda e, g=g, k=k, kk=kk: e.matmul(
                    P[1][0:1, :], ONE128B[:, :], X2G[g % 2][:, k * T:(k + 1) * T],
                    start=(kk == 0), stop=(kk == KT - 1)),
                    sem=("tP" if k == G - 1 else None))
            st["x2_pe"][g % 2] = stat_tp
        S.wait("vector", "tP", stat_tp)
        MU, EX2, VAR, RVAR = (STAT[0:1, i * T:(i + 1) * T] for i in range(4))
        S.emit("vector", lambda e: e.tensor_scalar_mul(MU, P[0][0:1, :], 1.0 / D))
        S.emit("vector", lambda e: e.tensor_scalar_mul(EX2, P[1][0:1, :], 1.0 / D))
        S.emit("vector", lambda e: e.tensor_mul(VAR, MU, MU))
        S.emit("vector", lambda e: e.tensor_sub(VAR, EX2, VAR))
        S.emit("vector", lambda e: e.tensor_scalar_add(VAR, VAR, 1e-5))
        v1 = S.emit("vector", lambda e: e.reciprocal(RVAR, VAR), sem="tV")
        pfree[0] = ("tV", v1)
        pfree[1] = ("tV", v1)
        S.wait("scalar", "tV", v1)
        a1 = S.emit("scalar", lambda e: e.activation(
            out=RSTD[0:1, :], in_=RVAR, func=AF.Sqrt), sem="tA")
        S.wait("tensor", "tA", a1)
        pe_psum(2)
        pe_psum(3)
        S.emit("tensor", lambda e: e.matmul(
            P[2][:, :], ONEF32[:, :], MU, start=True, stop=True))
        rep_tp = S.emit("tensor", lambda e: e.matmul(
            P[3][:, :], ONEF32[:, :], RSTD[0:1, :], start=True, stop=True), sem="tP")
        S.wait("vector", "tP", rep_tp)
        S.wait("vector", "tA", ta)
        mub = AP(P[2], 0, [[T, 128], [0, G], [1, T]])
        rsb = AP(P[3], 0, [[T, 128], [0, G], [1, T]])
        for g in range(NG):
            lo, hi = g * G * T, (g + 1) * G * T
            xg = XN[:, lo:hi].rearrange("p (g t) -> p g t", t=T)
            x2g = X2G[g % 2][:, :].rearrange("p (g t) -> p g t", t=T)
            S.emit("vector", lambda e, x2g=x2g, xg=xg, mub=mub: e.tensor_tensor(
                x2g, xg, mub, ALU.subtract))
            tv = S.emit("vector", lambda e, xg=xg, x2g=x2g, rsb=rsb: e.tensor_tensor(
                xg, x2g, rsb, ALU.mult), sem="tV")
            for k in range(g * G, (g + 1) * G):
                st["xn_tv"][k] = tv
        st["x2_tv"] = st["xn_tv"][KT - 1]
        st["eluq_wait"] = ("tV", st["x2_tv"])
        pfree[2] = ("tV", st["xn_tv"][KT - 1])
        pfree[3] = ("tV", st["xn_tv"][KT - 1])

    def stream_strip(kind, first, idx, par, nkt):
        scratch = {"w0": w0s, "wq": wqs, "wo": wos}[kind]
        if first:
            if kind == "wo":
                for hf in range(2):
                    src = AP(wot, (hf * 16 * 128) * D + idx * 128,
                             [[D, 128], [128 * D, 16], [1, 128]])
                    dst = WF[hf][:, 0:2048].rearrange("p (k f) -> p k f", f=128)
                    S.wait("sync", "tA", st["wf_ta"][hf])
                    qs = S.emit("sync", lambda e, s=src, d=dst: e.dma_start(
                        out=d, in_=s), sem="qS", by=16)
                    S.wait("scalar", "qS", qs)
                    S.wait("scalar", *st["ms_free"][hf])
                    ta = S.emit("scalar", lambda e, hf=hf: e.copy(
                        out=MS[hf][:, 0:2048], in_=WF[hf][:, 0:2048]), sem="tA")
                    st["wf_ta"][hf] = ta
                S.wait("tensor", "tA", ta)
                pe_psum(5)
                pe_psum(6)
                fold_tp = 0
                for j in range(HKV):
                    for g in range(4):
                        b = 4 * j + g
                        fold_tp = S.emit("tensor", lambda e, j=j, g=g, b=b: e.matmul(
                            P[5 + j // 4][:, (j % 4) * 128:(j % 4 + 1) * 128],
                            IDB[:, :],
                            MS[b // 16][:, (b % 16) * 128:(b % 16 + 1) * 128],
                            start=(g == 0), stop=(g == 3)),
                            sem=("tP" if (j == HKV - 1 and g == 3) else None))
                st["ms_free"][0] = ("tP", fold_tp)
                st["ms_free"][1] = ("tP", fold_tp)
                S.wait("scalar", "tP", fold_tp)
                S.wait("scalar", "tP", st["ws_pe"][par])
                S.wait("scalar", "qG", st["ws_wb_par"][par])
                S.emit("scalar", lambda e, par=par: e.copy(
                    out=WS[par][:, 0:512], in_=P[5][:, :]))
                ta = S.emit("scalar", lambda e, par=par: e.copy(
                    out=WS[par][:, 512:1024], in_=P[6][:, :]), sem="tA")
                pfree[5] = ("tA", ta)
                pfree[6] = ("tA", ta)
            else:
                src_t, width = (w0t, D) if kind == "w0" else (wqt, 1024)
                ta = 0
                for hf in range(2):
                    src = AP(src_t, (hf * 16 * 128) * width + idx * 128,
                             [[width, 128], [128 * width, 16], [1, 128]])
                    dst = WF[hf][:, 0:2048].rearrange("p (k f) -> p k f", f=128)
                    S.wait("sync", "tA", st["wf_ta"][hf])
                    qs = S.emit("sync", lambda e, s=src, d=dst: e.dma_start(
                        out=d, in_=s), sem="qS", by=16)
                    S.wait("scalar", "qS", qs)
                    if hf == 0:
                        S.wait("scalar", "tP", st["ws_pe"][par])
                        S.wait("scalar", "qG", st["ws_wb_par"][par])
                    scl = 0.5 if kind == "w0" else 1.0
                    ta = S.emit("scalar", lambda e, hf=hf, par=par, scl=scl: e.activation(
                        out=WS[par][:, hf * 2048:(hf + 1) * 2048],
                        in_=WF[hf][:, 0:2048], func=AF.Identity, scale=scl), sem="tA")
                    st["wf_ta"][hf] = ta
            S.wait("gpsimd", "tA", ta)
            dstw = AP(scratch, idx * nkt * 128 * 128,
                      [[128, 128], [128 * 128, nkt], [1, 128]])
            srcw = WS[par][:, 0:nkt * 128].rearrange("p (k f) -> p k f", f=128)
            wb = S.emit("gpsimd", lambda e, d=dstw, s=srcw: e.dma_start(out=d, in_=s),
                        sem="qG", by=16)
            st["ws_wb_qg"][(kind, idx)] = wb
            st["ws_wb_par"][par] = wb
            return ("tA", ta)
        else:
            S.wait("sync", "tP", st["ws_pe"][par])
            S.wait("sync", "qG", max(st["ws_wb_qg"].get((kind, idx), 0),
                                     st["ws_wb_par"][par]))
            src = AP(scratch, idx * nkt * 128 * 128,
                     [[128, 128], [128 * 128, nkt], [1, 128]])
            dst = WS[par][:, 0:nkt * 128].rearrange("p (k f) -> p k f", f=128)
            qs = S.emit("sync", lambda e, s=src, d=dst: e.dma_start(out=d, in_=s),
                        sem="qS", by=16)
            return ("qS", qs)

    def ffn_phase(chunk, call):
        pe_psum(5)
        h_tp = 0
        for k in range(KT):
            S.wait("tensor", "tV", st["xn_tv"][k])
            h_tp = S.emit("tensor", lambda e, k=k: e.matmul(
                P[5][:, :], AT[:, k * 128:(k + 1) * 128], XN[:, k * T:(k + 1) * T],
                start=(k == 0), stop=(k == KT - 1)), sem="tP")
        S.wait("scalar", "tP", h_tp)
        hb_ta = S.emit("scalar", lambda e: e.copy(out=HB[:, :], in_=P[5][:, :]), sem="tA")
        pfree[5] = ("tA", hb_ta)

        firstpass = (chunk == 0 and call == 1)
        for o in range(OT):
            par = o % 2
            sem, val = stream_strip("w0", firstpass, o, par, KT)
            S.wait("gpsimd", *st["ms_free"][par])
            msrc = AP(m4_d, (o * 4 * 128) * TPC + chunk * T,
                      [[TPC, 128], [128 * TPC, 4], [1, T]])
            mdst = MS[par][:, :].rearrange("p (l t) -> p l t", t=T)
            mq = S.emit("gpsimd", lambda e, s=msrc, d=mdst: e.dma_start(out=d, in_=s),
                        sem="qG", by=16)
            S.wait("tensor", sem, val)
            pe_psum(0)
            for i in range(1, 5):
                pe_psum(i)
            S.wait("tensor", "tA", hb_ta)
            for k in range(KT):
                S.emit("tensor", lambda e, k=k, par=par: e.matmul(
                    P[0][:, :], WS[par][:, k * 128:(k + 1) * 128],
                    XN[:, k * T:(k + 1) * T],
                    start=(k == 0), stop=(k == KT - 1)))
            pe_o = 0
            for l in range(4):
                pe_o = S.emit("tensor", lambda e, l=l, o=o: e.matmul(
                    P[1 + l][:, :],
                    BTs[32 * l:32 * (l + 1), o * 128:(o + 1) * 128],
                    HB[32 * l:32 * (l + 1), :],
                    start=True, stop=True, tile_position=(32 * l, 0)),
                    sem=("tP" if l == 3 else None))
            st["ws_pe"][par] = pe_o
            if o == OT - 1:
                st["xn_read_tp"] = pe_o
            S.wait("scalar", "tP", pe_o)
            S.wait("scalar", "tV", st["bpr_free_tv"][par])
            S.emit("scalar", lambda e, o=o, par=par: e.activation(
                out=BS[par][:, :], in_=P[0][:, :], func=AF.Identity,
                bias=B0H[:, o:o + 1]))
            for l in range(4):
                last = S.emit("scalar", lambda e, l=l, par=par: e.copy(
                    out=YS[par][:, l * T:(l + 1) * T], in_=P[1 + l][:, :]),
                    sem=("tA" if l == 3 else None))
            evac_ta = last
            for i in range(5):
                pfree[i] = ("tA", evac_ta)
            S.wait("vector", "tA", evac_ta)
            S.wait("vector", "qG", mq)
            S.emit("vector", lambda e, par=par: e.tensor_mul(
                TACC[:, :], YS[par][:, 0:T], MS[par][:, 0:T]))
            for l in range(1, 4):
                S.emit("vector", lambda e, l=l, par=par: e.tensor_mul(
                    TMUL[:, :], YS[par][:, l * T:(l + 1) * T],
                    MS[par][:, l * T:(l + 1) * T]))
                S.emit("vector", lambda e: e.tensor_add(TACC[:, :], TACC[:, :], TMUL[:, :]))
            S.emit("vector", lambda e, par=par: e.tensor_add(
                TACC[:, :], TACC[:, :], BS[par][:, :]))
            tv = S.emit("vector", lambda e, o=o: e.tensor_add(
                RS[:, o * T:(o + 1) * T], RS[:, o * T:(o + 1) * T], TACC[:, :]),
                sem="tV")
            st["ms_free"][par] = ("tV", tv)
            st["bpr_free_tv"][par] = tv
            st["last_rs_tv"][o] = tv
            if call == 2:
                S.wait("gpsimd", "tV", tv)
                dsto = AP(out_d, (o * 128) * TPC + chunk * T, [[TPC, 128], [1, T]])
                S.emit("gpsimd", lambda e, d=dsto, o=o: e.dma_start(
                    out=d, in_=RS[:, o * T:(o + 1) * T]), sem="qG", by=16)

    def attn_phase(chunk):
        first = (chunk == 0)
        eluq_tv = 0
        for hh in range(HKV):
            par = hh % 2
            sem, val = stream_strip("wq", first, hh, par, KT)
            S.wait("tensor", sem, val)
            pe_psum(0)
            q_tp = 0
            for k in range(KT):
                S.wait("tensor", "tV", st["xn_tv"][k])
                q_tp = S.emit("tensor", lambda e, k=k, par=par: e.matmul(
                    P[0][:, :], WS[par][:, k * 128:(k + 1) * 128],
                    XN[:, k * T:(k + 1) * T],
                    start=(k == 0), stop=(k == KT - 1)),
                    sem=("tP" if k == KT - 1 else None))
            st["ws_pe"][par] = q_tp
            if hh == HKV - 1:
                st["xn_read_tp"] = q_tp
            S.wait("scalar", "tP", q_tp)
            S.wait("scalar", *st["eluq_wait"])
            S.emit("scalar", lambda e, hh=hh: e.activation(
                out=X2[0][:, :], in_=P[0][:, :], func=AF.Relu,
                bias=BQN[:, hh:hh + 1], scale=-1.0))
            S.emit("scalar", lambda e: e.activation(
                out=X2[1][:, :], in_=X2[0][:, :], func=AF.Exp, scale=-1.0))
            elu_ta = S.emit("scalar", lambda e, hh=hh: e.activation(
                out=X2[0][:, :], in_=P[0][:, :], func=AF.Relu,
                bias=BQC[:, hh:hh + 1]), sem="tA")
            pfree[0] = ("tA", elu_ta)
            S.wait("vector", "tA", elu_ta)
            if hh == 0:
                S.wait("vector", "tP", st["eq_read_tp"])
            eluq_tv = S.emit("vector", lambda e, hh=hh: e.tensor_add(
                EQ[:, hh * T:(hh + 1) * T], X2[1][:, :], X2[0][:, :]), sem="tV")
            st["eluq_wait"] = ("tV", eluq_tv)
        st["x2_tv"] = eluq_tv
        S.wait("tensor", "tV", eluq_tv)
        pe_psum(3)
        den_tp = 0
        for h in range(HKV):
            den_tp = S.emit("tensor", lambda e, h=h: e.matmul(
                P[3][0:HKV, :], ZBD[:, h * 8:(h + 1) * 8], EQ[:, h * T:(h + 1) * T],
                start=(h == 0), stop=(h == HKV - 1)),
                sem=("tP" if h == HKV - 1 else None))
        S.wait("vector", "tP", den_tp)
        S.emit("vector", lambda e: e.tensor_scalar(
            DEN[:, :], P[3][0:HKV, :], 1e-6, None, op0=ALU.add))
        rden_tv = S.emit("vector", lambda e: e.reciprocal(RDEN[:, :], DEN[:, :]), sem="tV")
        pfree[3] = ("tV", rden_tv)
        retr_tv = 0
        for h in range(HKV):
            pe_psum(1)
            pe_psum(2)
            S.emit("tensor", lambda e, h=h: e.matmul(
                P[1][:, :], MSB[:, h * 128:(h + 1) * 128], EQ[:, h * T:(h + 1) * T],
                start=True, stop=True))
            S.wait("tensor", "tV", rden_tv)
            num_tp = S.emit("tensor", lambda e, h=h: e.matmul(
                P[2][:, :], EH[:, h * 128:(h + 1) * 128], RDEN[:, :],
                start=True, stop=True), sem="tP")
            S.wait("scalar", "tP", num_tp)
            rrep_ta = S.emit("scalar", lambda e: e.copy(out=RRb[:, :], in_=P[2][:, :]),
                             sem="tA")
            pfree[2] = ("tA", rrep_ta)
            S.wait("vector", "tP", num_tp)
            S.wait("vector", "tA", rrep_ta)
            retr_tv = S.emit("vector", lambda e, h=h: e.tensor_mul(
                EQ[:, h * T:(h + 1) * T], P[1][:, :], RRb[:, :]), sem="tV")
            pfree[1] = ("tV", retr_tv)
        S.wait("tensor", "tV", retr_tv)
        wo_tp = 0
        for o in range(OT):
            par = o % 2
            sem, val = stream_strip("wo", first, o, par, HKV)
            S.wait("tensor", sem, val)
            pe_psum(4)
            for k in range(HKV):
                wo_tp = S.emit("tensor", lambda e, k=k, par=par: e.matmul(
                    P[4][:, :], WS[par][:, k * 128:(k + 1) * 128],
                    EQ[:, k * T:(k + 1) * T],
                    start=(k == 0), stop=(k == HKV - 1)),
                    sem=("tP" if k == HKV - 1 else None))
            st["ws_pe"][par] = wo_tp
            S.wait("vector", "tP", wo_tp)
            tv = S.emit("vector", lambda e, o=o: e.tensor_add(
                RS[:, o * T:(o + 1) * T], RS[:, o * T:(o + 1) * T], P[4][:, :]),
                sem="tV")
            pfree[4] = ("tV", tv)
            st["last_rs_tv"][o] = tv
        st["eq_read_tp"] = wo_tp
        st["xn_read_tp"] = max(st["xn_read_tp"], wo_tp)

    stop = globals().get("DEBUG_STOP")
    nch = globals().get("DEBUG_NCH", NCH)
    done = False
    for c in range(nch):
        if done:
            break
        for k in range(KT):
            src = AP(xt, (k * 128) * TPC + c * T, [[TPC, 128], [1, T]])
            st["last_rs_qg"][k] = S.emit("gpsimd", lambda e, s=src, k=k: e.dma_start(
                out=RS[:, k * T:(k + 1) * T], in_=s), sem="qG", by=16,
                fence=(k == KT - 1))
        for name, fn in (("ln1", lambda: ln_phase(c, 0)),
                         ("ffn1", lambda: ffn_phase(c, 1)),
                         ("ln2", lambda: ln_phase(c, 1)),
                         ("attn", lambda: attn_phase(c)),
                         ("ln3", lambda: ln_phase(c, 2)),
                         ("ffn2", lambda: ffn_phase(c, 2))):
            fn()
            if stop == name:
                done = True
                break

    if stop is not None:
        S.wait("gpsimd", "tP", S.cnt["tP"])
        S.wait("gpsimd", "tA", S.cnt["tA"])
        S.wait("gpsimd", "tV", S.cnt["tV"])
        for k in range(KT):
            dsto = AP(out_d, (k * 128) * TPC, [[TPC, 128], [1, T]])
            S.emit("gpsimd", lambda e, d=dsto, k=k: e.dma_start(
                out=d, in_=RS[:, k * T:(k + 1) * T]), sem="qG", by=16,
                fence=(k == KT - 1))
        for k in range(KT):
            dsto = AP(out_d, (k * 128) * TPC + T, [[TPC, 128], [1, T // 2]])
            S.emit("gpsimd", lambda e, d=dsto, k=k: e.dma_start(
                out=d, in_=XN[:, k * T:(k + 1) * T].bitcast(F32)), sem="qG", by=16,
                fence=(k == KT - 1))
        S.emit("gpsimd", lambda e: e.dma_start(
            out=AP(out_d, 300 * TPC, [[TPC, 1], [1, T]]), in_=STAT[0:1, 0:T]), sem="qG", by=16)
        S.emit("gpsimd", lambda e: e.dma_start(
            out=AP(out_d, 301 * TPC, [[TPC, 1], [1, T]]), in_=RSTD[0:1, :]), sem="qG", by=16)
        S.emit("gpsimd", lambda e: e.dma_start(
            out=AP(out_d, 302 * TPC, [[TPC, 1], [1, T]]),
            in_=STAT[0:1, 3 * T:4 * T]), sem="qG", by=16)

    S.wait("gpsimd", "qG", S.cnt["qG"])

    with nc.Block() as block:
        @block.tensor
        def _(e):
            for f in S.prog["tensor"]:
                f(e)

        @block.scalar
        def _(e):
            for f in S.prog["scalar"]:
                f(e)

        @block.vector
        def _(e):
            for f in S.prog["vector"]:
                f(e)

        @block.sync
        def _(e):
            for f in S.prog["sync"]:
                f(e)

        @block.gpsimd
        def _(e):
            for f in S.prog["gpsimd"]:
                f(e)

    return nc


def kernel(L0_attn_hidden_states, L0_attn_Wq, L0_attn_bq, L0_attn_Wk, L0_attn_bk,
           L0_attn_Wv, L0_attn_bv, L0_attn_Wo, L0_attn_M, L0_attn_z,
           L0_ffn_W0, L0_ffn_b0, L0_ffn_A, L0_ffn_B):
    x = np.asarray(L0_attn_hidden_states, dtype=np.float32)
    Wq = np.asarray(L0_attn_Wq, dtype=np.float32)
    bq = np.asarray(L0_attn_bq, dtype=np.float32)
    Wo = np.asarray(L0_attn_Wo, dtype=np.float32)
    M = np.asarray(L0_attn_M, dtype=np.float32)
    z = np.asarray(L0_attn_z, dtype=np.float32)
    W0 = np.asarray(L0_ffn_W0, dtype=np.float32)
    b0 = np.asarray(L0_ffn_b0, dtype=np.float32)
    A = np.asarray(L0_ffn_A, dtype=np.float32)
    B = np.asarray(L0_ffn_B, dtype=np.float32)

    xt_full = np.ascontiguousarray(x.reshape(16384, D).T)
    w0t = np.ascontiguousarray(W0.T)
    sel_rows = np.concatenate([np.arange(h * 128, (h + 1) * 128)
                               for h in range(0, 32, 4)])
    wqt = np.ascontiguousarray(Wq[sel_rows].T)
    bq_sel = bq[sel_rows]
    wot = np.ascontiguousarray(Wo.T)
    atp = np.zeros((128, D), np.float32)
    for l in range(4):
        atp[32 * l:32 * l + 16] = A[l]
    at_h = np.ascontiguousarray(atp.T)
    btp = np.zeros((128, D), np.float32)
    for l in range(4):
        btp[32 * l:32 * l + 16] = B[l].T
    bt_h = np.ascontiguousarray(btp)
    zbd_h = np.zeros((128, HKV * 8), np.float32)
    for h in range(HKV):
        zbd_h[:, h * 8 + h] = z[h]
    eh_h = np.zeros((HKV, HKV * 128), np.float32)
    for h in range(HKV):
        eh_h[h, h * 128:(h + 1) * 128] = 1.0
    b0c = np.ascontiguousarray(b0.reshape(32, 128).T)
    bqc = np.ascontiguousarray(bq_sel.reshape(8, 128).T)

    sel = _mask_sel()

    nc = build_program()

    in_maps = []
    for c in range(N_CORES):
        t0, t1 = c * TPC, (c + 1) * TPC
        xt_c = np.ascontiguousarray(xt_full[:, t0:t1])
        blk = sel[t0:t1, :]
        m4 = np.empty((OT, 4, 128, TPC), dtype=ml_dtypes.bfloat16)
        for l in range(4):
            arr = (blk == l).T.reshape(OT, 128, TPC)
            m4[:, l] = (arr.astype(np.float32) * 4.0).astype(ml_dtypes.bfloat16)
        in_maps.append({
            "xt": xt_c, "w0t": w0t, "wqt": wqt, "wot": wot,
            "at": at_h, "bt": bt_h, "m": M, "zbd": zbd_h, "eh": eh_h,
            "id128": np.eye(128, dtype=np.float32),
            "b0c": b0c, "bqc": bqc, "m4": m4,
        })

    res = run_bass_kernel_spmd(nc, in_maps, core_ids=list(range(N_CORES)))
    outs = [np.asarray(r["out"]) for r in res.results]
    full = np.concatenate(outs, axis=1)
    return np.ascontiguousarray(full.T).reshape(4, 4096, 4096)


# version 06
# speedup vs baseline: 1.0832x; 1.0629x over previous
import os
import numpy as np
import ml_dtypes

import concourse.bass as bass
import concourse.mybir as mybir
from concourse.bass_utils import run_bass_kernel_spmd

F32 = mybir.dt.float32
BF16 = mybir.dt.bfloat16
AF = mybir.ActivationFunctionType
ALU = mybir.AluOpType

N_CORES = 8
D = 4096
KT = 32
TPC = 2048
NCH = 4
T = 512
OT = 32
HKV = 8
ENGS = ("tensor", "scalar", "vector", "sync", "gpsimd")


def _mask_sel():
    import jax
    import jax.numpy as jnp

    use64 = bool(os.environ.get("BASS_SIM_FP64"))
    if use64:
        jax.config.update("jax_enable_x64", True)
    cpu = jax.devices("cpu")[0]
    with jax.default_device(cpu):
        key = jax.random.key(42, impl="threefry2x32")
        dt = jnp.float64 if use64 else jnp.float32
        u = np.asarray(jax.random.uniform(key, (4, 4096, 4096), dtype=dt))
    sel = (u * 4).astype(np.uint8)
    return sel.reshape(16384, 4096)


class Sched:

    def __init__(self, sems):
        self.sems = sems
        self.prog = {e: [] for e in ENGS}
        self.cnt = {k: 0 for k in sems}
        self.last_wait = {e: {k: 0 for k in sems} for e in ENGS}

    def wait(self, eng, sem, val):
        if val <= 0:
            return
        assert val <= self.cnt[sem], f"wait {sem}>={val} > produced {self.cnt[sem]}"
        if self.last_wait[eng][sem] >= val:
            return
        self.last_wait[eng][sem] = val
        h = self.sems[sem]
        self.prog[eng].append(lambda e, h=h, v=val: e.wait_ge(h, v))

    def inc(self, sem, by):
        self.cnt[sem] += by
        return self.cnt[sem]

    def emit(self, eng, make, sem=None, by=1, fence=None):
        if eng in ("vector", "scalar"):
            self.prog[eng].append(lambda e: e.drain())
        if sem is None:
            self.prog[eng].append(make)
            return None
        n = self.inc(sem, by)
        h = self.sems[sem]
        self.prog[eng].append(lambda e, h=h, b=by: make(e).then_inc(h, b))
        if fence is None:
            fence = sem in ("qS", "qG")
        if fence and sem in ("qS", "qG"):
            self.wait(eng, sem, n)
        return n


def build_program():
    nc = bass.Bass("TRN2", target_bir_lowering=False, debug=False)

    xt = nc.dram_tensor("xt", [D, TPC], F32, kind="ExternalInput")
    w0t = nc.dram_tensor("w0t", [D, D], F32, kind="ExternalInput")
    wqt = nc.dram_tensor("wqt", [D, 1024], F32, kind="ExternalInput")
    wot = nc.dram_tensor("wot", [D, D], F32, kind="ExternalInput")
    at_d = nc.dram_tensor("at", [D, 128], F32, kind="ExternalInput")
    bt_d = nc.dram_tensor("bt", [128, D], F32, kind="ExternalInput")
    m_d = nc.dram_tensor("m", [HKV, 128, 128], F32, kind="ExternalInput")
    zbd_d = nc.dram_tensor("zbd", [128, HKV * 8], F32, kind="ExternalInput")
    eh_d = nc.dram_tensor("eh", [HKV, HKV * 128], F32, kind="ExternalInput")
    b0c_d = nc.dram_tensor("b0c", [128, 32], F32, kind="ExternalInput")
    bqc_d = nc.dram_tensor("bqc", [128, 8], F32, kind="ExternalInput")
    m4_d = nc.dram_tensor("m4", [OT, 4, 128, TPC], BF16, kind="ExternalInput")
    out_d = nc.dram_tensor("out", [D, TPC], F32, kind="ExternalOutput")
    w0s = nc.dram_tensor("w0s", [OT, KT, 128, 128], BF16, kind="Internal")
    wqs = nc.dram_tensor("wqs", [HKV, KT, 128, 128], BF16, kind="Internal")
    wos = nc.dram_tensor("wos", [OT, HKV, 128, 128], BF16, kind="Internal")

    RS = nc.alloc_sbuf_tensor("RS", [128, KT * T], F32)
    XN = nc.alloc_sbuf_tensor("XN", [128, KT * T], BF16)
    WS = [nc.alloc_sbuf_tensor(f"WS{i}", [128, KT * 128], BF16) for i in range(2)]
    MS = [nc.alloc_sbuf_tensor(f"MS{i}", [128, 4 * T], BF16) for i in range(2)]
    WF = [nc.alloc_sbuf_tensor(f"WF{i}", [128, 2048], F32) for i in range(2)]
    AT = nc.alloc_sbuf_tensor("AT", [128, KT * 128], BF16)
    BTs = nc.alloc_sbuf_tensor("BTs", [128, D], BF16)
    HB = nc.alloc_sbuf_tensor("HB", [128, T], BF16)
    X2G = [nc.alloc_sbuf_tensor(f"X2G{i}", [128, 8 * T], BF16) for i in range(2)]
    X2 = [X2G[0][:, 0:T], X2G[0][:, T:2 * T]]
    EQ = X2G[1]
    YS = [nc.alloc_sbuf_tensor(f"YS{i}", [128, 4 * T], BF16) for i in range(2)]
    BS = [nc.alloc_sbuf_tensor(f"BS{i}", [128, T], BF16) for i in range(2)]
    RRb = nc.alloc_sbuf_tensor("RRb", [128, T], BF16)
    MSB = nc.alloc_sbuf_tensor("MSB", [128, HKV * 128], BF16)
    ZBD = nc.alloc_sbuf_tensor("ZBD", [128, HKV * 8], BF16)
    EH = nc.alloc_sbuf_tensor("EH", [HKV, HKV * 128], F32)
    DEN = nc.alloc_sbuf_tensor("DEN", [HKV, T], F32)
    RDEN = nc.alloc_sbuf_tensor("RDEN", [HKV, T], F32)
    STAT = nc.alloc_sbuf_tensor("STAT", [1, 4 * T], F32)
    RSTD = nc.alloc_sbuf_tensor("RSTD", [1, T], F32)
    ONE128B = nc.alloc_sbuf_tensor("ONE128B", [128, 1], BF16)
    ONEF32 = nc.alloc_sbuf_tensor("ONEF32", [1, 128], F32)
    B0H = nc.alloc_sbuf_tensor("B0H", [128, 32], F32)
    BQC = nc.alloc_sbuf_tensor("BQC", [128, 8], F32)
    BQN = nc.alloc_sbuf_tensor("BQN", [128, 8], F32)
    TACC = nc.alloc_sbuf_tensor("TACC", [128, 4 * T], BF16)
    TMUL = nc.alloc_sbuf_tensor("TMUL", [128, 2 * T], BF16)

    P0 = nc.alloc_psum_tensor("P0", [128, T], F32)
    PY = nc.alloc_psum_tensor("PY", [128, 4 * T], F32)
    PH = nc.alloc_psum_tensor("PH", [128, T], F32)
    PB = nc.alloc_psum_tensor("PB", [128, T], F32)
    PC = nc.alloc_psum_tensor("PC", [128, T], F32)

    sems = {n: nc.alloc_semaphore(n) for n in ("qS", "qG", "tP", "tA", "tV")}
    S = Sched(sems)
    AP = bass.AP

    pfree = {k: ("tV", 0) for k in ("P0", "PY", "PH", "PB", "PC")}

    def pe_psum(i):
        sem, val = pfree[i]
        S.wait("tensor", sem, val)

    id_d = nc.dram_tensor("id128", [128, 128], F32, kind="ExternalInput")
    IDB = nc.alloc_sbuf_tensor("IDB", [128, 128], BF16)

    st = dict(
        last_rs_tv=[0] * KT,
        last_rs_qg=[0] * KT,
        xn_tv=[0] * KT,
        xn_read_tp=0,
        x2_tv=0,
        x2_pe=[0, 0],
        ws_pe=[0, 0],
        ws_wb_qg={},
        ws_wb_par=[0, 0],
        ms_free=[("tV", 0), ("tV", 0)],
        wf_ta=[0, 0],
        bpr_free_tv=[0, 0],
        eq_read_tp=0,
        eluq_wait=("tV", 0),
    )

    S.emit("vector", lambda e: e.memset(ONE128B[:, :], 1.0))
    S.emit("vector", lambda e: e.memset(ONEF32[:, :], 1.0))

    S.emit("gpsimd", lambda e: e.dma_start(out=B0H[:, :], in_=b0c_d[:]), sem="qG", by=16)
    S.emit("gpsimd", lambda e: e.dma_start(out=BQC[:, :], in_=bqc_d[:]), sem="qG", by=16)
    S.emit("gpsimd", lambda e: e.dma_start(out=EH[:, :], in_=eh_d[:]), sem="qG", by=16)
    g_zt = S.emit("gpsimd", lambda e: e.dma_start(out=WF[1][:, 0:HKV * 8], in_=zbd_d[:]),
                  sem="qG", by=16)
    ta = 0
    for hf in range(2):
        src = AP(at_d, hf * 16 * 128 * 128, [[128, 128], [128 * 128, 16], [1, 128]])
        dst = WF[0][:, 0:2048].rearrange("p (k f) -> p k f", f=128)
        S.wait("gpsimd", "tA", ta)
        qg = S.emit("gpsimd", lambda e, s=src, d=dst: e.dma_start(out=d, in_=s),
                    sem="qG", by=16)
        S.wait("scalar", "qG", qg)
        ta = S.emit("scalar", lambda e, hf=hf: e.copy(
            out=AT[:, hf * 2048:(hf + 1) * 2048], in_=WF[0][:, 0:2048]), sem="tA")
    S.wait("scalar", "qG", g_zt)
    ta = S.emit("scalar", lambda e: e.copy(
        out=ZBD[:, :], in_=WF[1][:, 0:HKV * 8]), sem="tA")
    for hf in range(2):
        src = AP(bt_d, hf * 2048, [[D, 128], [1, 2048]])
        S.wait("gpsimd", "tA", ta)
        qg = S.emit("gpsimd", lambda e, s=src: e.dma_start(out=WF[1][:, 0:2048], in_=s),
                    sem="qG", by=16)
        S.wait("scalar", "qG", qg)
        ta = S.emit("scalar", lambda e, hf=hf: e.copy(
            out=BTs[:, hf * 2048:(hf + 1) * 2048], in_=WF[1][:, 0:2048]), sem="tA")
    for h in range(HKV):
        src = AP(m_d, h * 128 * 128, [[128, 128], [1, 128]])
        S.wait("gpsimd", "tA", ta)
        qg = S.emit("gpsimd", lambda e, s=src: e.dma_start(out=WF[0][:, 0:128], in_=s),
                    sem="qG", by=16)
        S.wait("scalar", "qG", qg)
        ta = S.emit("scalar", lambda e, h=h: e.copy(
            out=MSB[:, h * 128:(h + 1) * 128], in_=WF[0][:, 0:128]), sem="tA")
    S.wait("gpsimd", "tA", ta)
    qg = S.emit("gpsimd", lambda e: e.dma_start(out=WF[0][:, 0:128], in_=id_d[:]),
                sem="qG", by=16)
    S.wait("scalar", "qG", qg)
    ta = S.emit("scalar", lambda e: e.copy(out=IDB[:, :], in_=WF[0][:, 0:128]), sem="tA")
    st["wf_ta"] = [ta, ta]
    S.wait("vector", "qG", g_zt)
    S.wait("tensor", "qG", g_zt)
    S.emit("vector", lambda e: e.tensor_scalar_mul(B0H[:, :], B0H[:, :], 0.5))
    setup_tv = S.emit("vector", lambda e: e.tensor_scalar_mul(
        BQN[:, :], BQC[:, :], -1.0), sem="tV")
    S.wait("tensor", "tV", setup_tv)
    S.wait("scalar", "tV", setup_tv)
    S.wait("sync", "tV", setup_tv)
    S.wait("gpsimd", "tV", setup_tv)

    def ln_phase(chunk, phase):
        G = 8
        NG = KT // G
        pe_psum("PH")
        pe_psum("PB")
        stat_tp = 0
        S.wait("scalar", *st["eluq_wait"])
        S.wait("scalar", "tP", st["xn_read_tp"])
        for g in range(NG):
            lo, hi = g * G * T, (g + 1) * G * T
            if phase == 0:
                S.wait("scalar", "qG", st["last_rs_qg"][KT - 1])
            else:
                for k in range(g * G, (g + 1) * G):
                    S.wait("scalar", "tV", st["last_rs_tv"][k])
            S.emit("scalar", lambda e, lo=lo, hi=hi: e.copy(
                out=XN[:, lo:hi], in_=RS[:, lo:hi]))
            S.wait("scalar", "tP", st["x2_pe"][g % 2])
            ta = S.emit("scalar", lambda e, g=g, lo=lo, hi=hi: e.activation(
                out=X2G[g % 2][:, :], in_=XN[:, lo:hi], func=AF.Square), sem="tA")
            S.wait("tensor", "tA", ta)
            for k in range(G):
                kk = g * G + k
                S.emit("tensor", lambda e, kk=kk: e.matmul(
                    PH[0:1, :], ONE128B[:, :], XN[:, kk * T:(kk + 1) * T],
                    start=(kk == 0), stop=(kk == KT - 1)))
                stat_tp = S.emit("tensor", lambda e, g=g, k=k, kk=kk: e.matmul(
                    PB[0:1, :], ONE128B[:, :], X2G[g % 2][:, k * T:(k + 1) * T],
                    start=(kk == 0), stop=(kk == KT - 1)),
                    sem=("tP" if k == G - 1 else None))
            st["x2_pe"][g % 2] = stat_tp
        S.wait("vector", "tP", stat_tp)
        MU, EX2, VAR, RVAR = (STAT[0:1, i * T:(i + 1) * T] for i in range(4))
        S.emit("vector", lambda e: e.tensor_scalar_mul(MU, PH[0:1, :], 1.0 / D))
        S.emit("vector", lambda e: e.tensor_scalar_mul(EX2, PB[0:1, :], 1.0 / D))
        S.emit("vector", lambda e: e.tensor_mul(VAR, MU, MU))
        S.emit("vector", lambda e: e.tensor_sub(VAR, EX2, VAR))
        S.emit("vector", lambda e: e.tensor_scalar_add(VAR, VAR, 1e-5))
        v1 = S.emit("vector", lambda e: e.reciprocal(RVAR, VAR), sem="tV")
        pfree["PH"] = ("tV", v1)
        pfree["PB"] = ("tV", v1)
        S.wait("scalar", "tV", v1)
        a1 = S.emit("scalar", lambda e: e.activation(
            out=RSTD[0:1, :], in_=RVAR, func=AF.Sqrt), sem="tA")
        S.wait("tensor", "tA", a1)
        pe_psum("PY")
        S.emit("tensor", lambda e: e.matmul(
            PY[:, 0:T], ONEF32[:, :], MU, start=True, stop=True))
        rep_tp = S.emit("tensor", lambda e: e.matmul(
            PY[:, T:2 * T], ONEF32[:, :], RSTD[0:1, :], start=True, stop=True), sem="tP")
        S.wait("vector", "tP", rep_tp)
        S.wait("vector", "tA", ta)
        mub = AP(PY, 0, [[4 * T, 128], [0, G], [1, T]])
        rsb = AP(PY, T, [[4 * T, 128], [0, G], [1, T]])
        for g in range(NG):
            lo, hi = g * G * T, (g + 1) * G * T
            xg = XN[:, lo:hi].rearrange("p (g t) -> p g t", t=T)
            x2g = X2G[g % 2][:, :].rearrange("p (g t) -> p g t", t=T)
            S.emit("vector", lambda e, x2g=x2g, xg=xg, mub=mub: e.tensor_tensor(
                x2g, xg, mub, ALU.subtract))
            tv = S.emit("vector", lambda e, xg=xg, x2g=x2g, rsb=rsb: e.tensor_tensor(
                xg, x2g, rsb, ALU.mult), sem="tV")
            for k in range(g * G, (g + 1) * G):
                st["xn_tv"][k] = tv
        st["x2_tv"] = st["xn_tv"][KT - 1]
        st["eluq_wait"] = ("tV", st["x2_tv"])
        pfree["PY"] = ("tV", st["xn_tv"][KT - 1])

    def stream_strip(kind, first, idx, par, nkt):
        scratch = {"w0": w0s, "wq": wqs, "wo": wos}[kind]
        if first:
            if kind == "wo":
                for hf in range(2):
                    src = AP(wot, (hf * 16 * 128) * D + idx * 128,
                             [[D, 128], [128 * D, 16], [1, 128]])
                    dst = WF[hf][:, 0:2048].rearrange("p (k f) -> p k f", f=128)
                    S.wait("sync", "tA", st["wf_ta"][hf])
                    qs = S.emit("sync", lambda e, s=src, d=dst: e.dma_start(
                        out=d, in_=s), sem="qS", by=16)
                    S.wait("scalar", "qS", qs)
                    S.wait("scalar", *st["ms_free"][hf])
                    ta = S.emit("scalar", lambda e, hf=hf: e.copy(
                        out=MS[hf][:, 0:2048], in_=WF[hf][:, 0:2048]), sem="tA")
                    st["wf_ta"][hf] = ta
                S.wait("tensor", "tA", ta)
                pe_psum("PY")
                fold_tp = 0
                for j in range(HKV):
                    for g in range(4):
                        b = 4 * j + g
                        fold_tp = S.emit("tensor", lambda e, j=j, g=g, b=b: e.matmul(
                            PY[:, (j // 4) * T + (j % 4) * 128:(j // 4) * T + (j % 4 + 1) * 128],
                            IDB[:, :],
                            MS[b // 16][:, (b % 16) * 128:(b % 16 + 1) * 128],
                            start=(g == 0), stop=(g == 3)),
                            sem=("tP" if (j == HKV - 1 and g == 3) else None))
                st["ms_free"][0] = ("tP", fold_tp)
                st["ms_free"][1] = ("tP", fold_tp)
                S.wait("scalar", "tP", fold_tp)
                S.wait("scalar", "tP", st["ws_pe"][par])
                S.wait("scalar", "qG", st["ws_wb_par"][par])
                S.emit("scalar", lambda e, par=par: e.copy(
                    out=WS[par][:, 0:512], in_=PY[:, 0:T]))
                ta = S.emit("scalar", lambda e, par=par: e.copy(
                    out=WS[par][:, 512:1024], in_=PY[:, T:2 * T]), sem="tA")
                pfree["PY"] = ("tA", ta)
            else:
                src_t, width = (w0t, D) if kind == "w0" else (wqt, 1024)
                ta = 0
                for hf in range(2):
                    src = AP(src_t, (hf * 16 * 128) * width + idx * 128,
                             [[width, 128], [128 * width, 16], [1, 128]])
                    dst = WF[hf][:, 0:2048].rearrange("p (k f) -> p k f", f=128)
                    S.wait("sync", "tA", st["wf_ta"][hf])
                    qs = S.emit("sync", lambda e, s=src, d=dst: e.dma_start(
                        out=d, in_=s), sem="qS", by=16)
                    S.wait("scalar", "qS", qs)
                    if hf == 0:
                        S.wait("scalar", "tP", st["ws_pe"][par])
                        S.wait("scalar", "qG", st["ws_wb_par"][par])
                    scl = 0.5 if kind == "w0" else 1.0
                    ta = S.emit("scalar", lambda e, hf=hf, par=par, scl=scl: e.activation(
                        out=WS[par][:, hf * 2048:(hf + 1) * 2048],
                        in_=WF[hf][:, 0:2048], func=AF.Identity, scale=scl), sem="tA")
                    st["wf_ta"][hf] = ta
            S.wait("gpsimd", "tA", ta)
            dstw = AP(scratch, idx * nkt * 128 * 128,
                      [[128, 128], [128 * 128, nkt], [1, 128]])
            srcw = WS[par][:, 0:nkt * 128].rearrange("p (k f) -> p k f", f=128)
            wb = S.emit("gpsimd", lambda e, d=dstw, s=srcw: e.dma_start(out=d, in_=s),
                        sem="qG", by=16)
            st["ws_wb_qg"][(kind, idx)] = wb
            st["ws_wb_par"][par] = wb
            return ("tA", ta)
        else:
            S.wait("sync", "tP", st["ws_pe"][par])
            S.wait("sync", "qG", max(st["ws_wb_qg"].get((kind, idx), 0),
                                     st["ws_wb_par"][par]))
            src = AP(scratch, idx * nkt * 128 * 128,
                     [[128, 128], [128 * 128, nkt], [1, 128]])
            dst = WS[par][:, 0:nkt * 128].rearrange("p (k f) -> p k f", f=128)
            qs = S.emit("sync", lambda e, s=src, d=dst: e.dma_start(out=d, in_=s),
                        sem="qS", by=16)
            return ("qS", qs)

    def ffn_phase(chunk, call):
        pe_psum("PH")
        h_tp = 0
        for k in range(KT):
            S.wait("tensor", "tV", st["xn_tv"][k])
            h_tp = S.emit("tensor", lambda e, k=k: e.matmul(
                PH[:, :], AT[:, k * 128:(k + 1) * 128], XN[:, k * T:(k + 1) * T],
                start=(k == 0), stop=(k == KT - 1)), sem="tP")
        S.wait("scalar", "tP", h_tp)
        hb_ta = S.emit("scalar", lambda e: e.copy(out=HB[:, :], in_=PH[:, :]), sem="tA")
        pfree["PH"] = ("tA", hb_ta)

        firstpass = (chunk == 0 and call == 1)
        for o in range(OT):
            par = o % 2
            sem, val = stream_strip("w0", firstpass, o, par, KT)
            S.wait("gpsimd", *st["ms_free"][par])
            msrc = AP(m4_d, (o * 4 * 128) * TPC + chunk * T,
                      [[TPC, 128], [128 * TPC, 4], [1, T]])
            mdst = MS[par][:, :].rearrange("p (l t) -> p l t", t=T)
            mq = S.emit("gpsimd", lambda e, s=msrc, d=mdst: e.dma_start(out=d, in_=s),
                        sem="qG", by=16)
            S.wait("tensor", sem, val)
            pe_psum("P0")
            pe_psum("PY")
            S.wait("tensor", "tA", hb_ta)
            for k in range(KT):
                S.emit("tensor", lambda e, k=k, par=par: e.matmul(
                    P0[:, :], WS[par][:, k * 128:(k + 1) * 128],
                    XN[:, k * T:(k + 1) * T],
                    start=(k == 0), stop=(k == KT - 1)))
            pe_o = 0
            for l in range(4):
                pe_o = S.emit("tensor", lambda e, l=l, o=o: e.matmul(
                    PY[:, l * T:(l + 1) * T],
                    BTs[32 * l:32 * (l + 1), o * 128:(o + 1) * 128],
                    HB[32 * l:32 * (l + 1), :],
                    start=True, stop=True, tile_position=(32 * l, 0)),
                    sem=("tP" if l == 3 else None))
            st["ws_pe"][par] = pe_o
            if o == OT - 1:
                st["xn_read_tp"] = pe_o
            S.wait("scalar", "tP", pe_o)
            S.wait("scalar", "tV", st["bpr_free_tv"][par])
            S.emit("scalar", lambda e, o=o, par=par: e.activation(
                out=BS[par][:, :], in_=P0[:, :], func=AF.Identity,
                bias=B0H[:, o:o + 1]))
            evac_ta = S.emit("scalar", lambda e, par=par: e.copy(
                out=YS[par][:, :], in_=PY[:, :]), sem="tA")
            pfree["P0"] = ("tA", evac_ta)
            pfree["PY"] = ("tA", evac_ta)
            S.wait("vector", "tA", evac_ta)
            S.wait("vector", "qG", mq)
            S.emit("vector", lambda e, par=par: e.tensor_mul(
                TACC[:, :], YS[par][:, :], MS[par][:, :]))
            S.emit("vector", lambda e: e.tensor_add(
                TMUL[:, :], TACC[:, 0:2 * T], TACC[:, 2 * T:4 * T]))
            S.emit("vector", lambda e: e.tensor_add(
                TACC[:, 0:T], TMUL[:, 0:T], TMUL[:, T:2 * T]))
            S.emit("vector", lambda e, par=par: e.tensor_add(
                TACC[:, T:2 * T], TACC[:, 0:T], BS[par][:, :]))
            tv = S.emit("vector", lambda e, o=o: e.tensor_add(
                RS[:, o * T:(o + 1) * T], RS[:, o * T:(o + 1) * T],
                TACC[:, T:2 * T]), sem="tV")
            st["ms_free"][par] = ("tV", tv)
            st["bpr_free_tv"][par] = tv
            st["last_rs_tv"][o] = tv
            if call == 2:
                S.wait("gpsimd", "tV", tv)
                dsto = AP(out_d, (o * 128) * TPC + chunk * T, [[TPC, 128], [1, T]])
                S.emit("gpsimd", lambda e, d=dsto, o=o: e.dma_start(
                    out=d, in_=RS[:, o * T:(o + 1) * T]), sem="qG", by=16)

    def attn_phase(chunk):
        first = (chunk == 0)
        eluq_tv = 0
        for hh in range(HKV):
            par = hh % 2
            sem, val = stream_strip("wq", first, hh, par, KT)
            QP = P0 if hh % 2 == 0 else PH
            qkey = "P0" if hh % 2 == 0 else "PH"
            S.wait("tensor", sem, val)
            pe_psum(qkey)
            q_tp = 0
            for k in range(KT):
                S.wait("tensor", "tV", st["xn_tv"][k])
                q_tp = S.emit("tensor", lambda e, k=k, par=par, QP=QP: e.matmul(
                    QP[:, :], WS[par][:, k * 128:(k + 1) * 128],
                    XN[:, k * T:(k + 1) * T],
                    start=(k == 0), stop=(k == KT - 1)),
                    sem=("tP" if k == KT - 1 else None))
            st["ws_pe"][par] = q_tp
            if hh == HKV - 1:
                st["xn_read_tp"] = q_tp
            S.wait("scalar", "tP", q_tp)
            S.wait("scalar", *st["eluq_wait"])
            S.emit("scalar", lambda e, hh=hh, QP=QP: e.activation(
                out=X2[0][:, :], in_=QP[:, :], func=AF.Relu,
                bias=BQN[:, hh:hh + 1], scale=-1.0))
            S.emit("scalar", lambda e: e.activation(
                out=X2[1][:, :], in_=X2[0][:, :], func=AF.Exp, scale=-1.0))
            elu_ta = S.emit("scalar", lambda e, hh=hh, QP=QP: e.activation(
                out=X2[0][:, :], in_=QP[:, :], func=AF.Relu,
                bias=BQC[:, hh:hh + 1]), sem="tA")
            pfree[qkey] = ("tA", elu_ta)
            S.wait("vector", "tA", elu_ta)
            if hh == 0:
                S.wait("vector", "tP", st["eq_read_tp"])
            eluq_tv = S.emit("vector", lambda e, hh=hh: e.tensor_add(
                EQ[:, hh * T:(hh + 1) * T], X2[1][:, :], X2[0][:, :]), sem="tV")
            st["eluq_wait"] = ("tV", eluq_tv)
        st["x2_tv"] = eluq_tv
        S.wait("tensor", "tV", eluq_tv)
        pe_psum("PC")
        den_tp = 0
        for h in range(HKV):
            den_tp = S.emit("tensor", lambda e, h=h: e.matmul(
                PC[0:HKV, :], ZBD[:, h * 8:(h + 1) * 8], EQ[:, h * T:(h + 1) * T],
                start=(h == 0), stop=(h == HKV - 1)),
                sem=("tP" if h == HKV - 1 else None))
        S.wait("vector", "tP", den_tp)
        S.emit("vector", lambda e: e.tensor_scalar(
            DEN[:, :], PC[0:HKV, :], 1e-6, None, op0=ALU.add))
        rden_tv = S.emit("vector", lambda e: e.reciprocal(RDEN[:, :], DEN[:, :]), sem="tV")
        pfree["PC"] = ("tV", rden_tv)
        retr_tv = 0
        for h in range(HKV):
            pe_psum("PB")
            pe_psum("PC")
            S.emit("tensor", lambda e, h=h: e.matmul(
                PB[:, :], MSB[:, h * 128:(h + 1) * 128], EQ[:, h * T:(h + 1) * T],
                start=True, stop=True))
            S.wait("tensor", "tV", rden_tv)
            num_tp = S.emit("tensor", lambda e, h=h: e.matmul(
                PC[:, :], EH[:, h * 128:(h + 1) * 128], RDEN[:, :],
                start=True, stop=True), sem="tP")
            S.wait("scalar", "tP", num_tp)
            rrep_ta = S.emit("scalar", lambda e: e.copy(out=RRb[:, :], in_=PC[:, :]),
                             sem="tA")
            pfree["PC"] = ("tA", rrep_ta)
            S.wait("vector", "tP", num_tp)
            S.wait("vector", "tA", rrep_ta)
            retr_tv = S.emit("vector", lambda e, h=h: e.tensor_mul(
                EQ[:, h * T:(h + 1) * T], PB[:, :], RRb[:, :]), sem="tV")
            pfree["PB"] = ("tV", retr_tv)
        S.wait("tensor", "tV", retr_tv)
        wo_tp = 0
        for o in range(OT):
            par = o % 2
            sem, val = stream_strip("wo", first, o, par, HKV)
            WP = P0 if o % 2 == 0 else PH
            wkey = "P0" if o % 2 == 0 else "PH"
            S.wait("tensor", sem, val)
            pe_psum(wkey)
            for k in range(HKV):
                wo_tp = S.emit("tensor", lambda e, k=k, par=par, WP=WP: e.matmul(
                    WP[:, :], WS[par][:, k * 128:(k + 1) * 128],
                    EQ[:, k * T:(k + 1) * T],
                    start=(k == 0), stop=(k == HKV - 1)),
                    sem=("tP" if k == HKV - 1 else None))
            st["ws_pe"][par] = wo_tp
            S.wait("vector", "tP", wo_tp)
            tv = S.emit("vector", lambda e, o=o, WP=WP: e.tensor_add(
                RS[:, o * T:(o + 1) * T], RS[:, o * T:(o + 1) * T], WP[:, :]),
                sem="tV")
            pfree[wkey] = ("tV", tv)
            st["last_rs_tv"][o] = tv
        st["eq_read_tp"] = wo_tp
        st["xn_read_tp"] = max(st["xn_read_tp"], wo_tp)

    stop = globals().get("DEBUG_STOP")
    nch = globals().get("DEBUG_NCH", NCH)
    done = False
    for c in range(nch):
        if done:
            break
        for k in range(KT):
            src = AP(xt, (k * 128) * TPC + c * T, [[TPC, 128], [1, T]])
            st["last_rs_qg"][k] = S.emit("gpsimd", lambda e, s=src, k=k: e.dma_start(
                out=RS[:, k * T:(k + 1) * T], in_=s), sem="qG", by=16,
                fence=(k == KT - 1))
        for name, fn in (("ln1", lambda: ln_phase(c, 0)),
                         ("ffn1", lambda: ffn_phase(c, 1)),
                         ("ln2", lambda: ln_phase(c, 1)),
                         ("attn", lambda: attn_phase(c)),
                         ("ln3", lambda: ln_phase(c, 2)),
                         ("ffn2", lambda: ffn_phase(c, 2))):
            fn()
            if stop == name:
                done = True
                break

    if stop is not None:
        S.wait("gpsimd", "tP", S.cnt["tP"])
        S.wait("gpsimd", "tA", S.cnt["tA"])
        S.wait("gpsimd", "tV", S.cnt["tV"])
        for k in range(KT):
            dsto = AP(out_d, (k * 128) * TPC, [[TPC, 128], [1, T]])
            S.emit("gpsimd", lambda e, d=dsto, k=k: e.dma_start(
                out=d, in_=RS[:, k * T:(k + 1) * T]), sem="qG", by=16,
                fence=(k == KT - 1))
        for k in range(KT):
            dsto = AP(out_d, (k * 128) * TPC + T, [[TPC, 128], [1, T // 2]])
            S.emit("gpsimd", lambda e, d=dsto, k=k: e.dma_start(
                out=d, in_=XN[:, k * T:(k + 1) * T].bitcast(F32)), sem="qG", by=16,
                fence=(k == KT - 1))
        S.emit("gpsimd", lambda e: e.dma_start(
            out=AP(out_d, 300 * TPC, [[TPC, 1], [1, T]]), in_=STAT[0:1, 0:T]), sem="qG", by=16)
        S.emit("gpsimd", lambda e: e.dma_start(
            out=AP(out_d, 301 * TPC, [[TPC, 1], [1, T]]), in_=RSTD[0:1, :]), sem="qG", by=16)
        S.emit("gpsimd", lambda e: e.dma_start(
            out=AP(out_d, 302 * TPC, [[TPC, 1], [1, T]]),
            in_=STAT[0:1, 3 * T:4 * T]), sem="qG", by=16)

    S.wait("gpsimd", "qG", S.cnt["qG"])

    with nc.Block() as block:
        @block.tensor
        def _(e):
            for f in S.prog["tensor"]:
                f(e)

        @block.scalar
        def _(e):
            for f in S.prog["scalar"]:
                f(e)

        @block.vector
        def _(e):
            for f in S.prog["vector"]:
                f(e)

        @block.sync
        def _(e):
            for f in S.prog["sync"]:
                f(e)

        @block.gpsimd
        def _(e):
            for f in S.prog["gpsimd"]:
                f(e)

    return nc


def kernel(L0_attn_hidden_states, L0_attn_Wq, L0_attn_bq, L0_attn_Wk, L0_attn_bk,
           L0_attn_Wv, L0_attn_bv, L0_attn_Wo, L0_attn_M, L0_attn_z,
           L0_ffn_W0, L0_ffn_b0, L0_ffn_A, L0_ffn_B):
    x = np.asarray(L0_attn_hidden_states, dtype=np.float32)
    Wq = np.asarray(L0_attn_Wq, dtype=np.float32)
    bq = np.asarray(L0_attn_bq, dtype=np.float32)
    Wo = np.asarray(L0_attn_Wo, dtype=np.float32)
    M = np.asarray(L0_attn_M, dtype=np.float32)
    z = np.asarray(L0_attn_z, dtype=np.float32)
    W0 = np.asarray(L0_ffn_W0, dtype=np.float32)
    b0 = np.asarray(L0_ffn_b0, dtype=np.float32)
    A = np.asarray(L0_ffn_A, dtype=np.float32)
    B = np.asarray(L0_ffn_B, dtype=np.float32)

    xt_full = np.ascontiguousarray(x.reshape(16384, D).T)
    w0t = np.ascontiguousarray(W0.T)
    sel_rows = np.concatenate([np.arange(h * 128, (h + 1) * 128)
                               for h in range(0, 32, 4)])
    wqt = np.ascontiguousarray(Wq[sel_rows].T)
    bq_sel = bq[sel_rows]
    wot = np.ascontiguousarray(Wo.T)
    atp = np.zeros((128, D), np.float32)
    for l in range(4):
        atp[32 * l:32 * l + 16] = A[l]
    at_h = np.ascontiguousarray(atp.T)
    btp = np.zeros((128, D), np.float32)
    for l in range(4):
        btp[32 * l:32 * l + 16] = B[l].T
    bt_h = np.ascontiguousarray(btp)
    zbd_h = np.zeros((128, HKV * 8), np.float32)
    for h in range(HKV):
        zbd_h[:, h * 8 + h] = z[h]
    eh_h = np.zeros((HKV, HKV * 128), np.float32)
    for h in range(HKV):
        eh_h[h, h * 128:(h + 1) * 128] = 1.0
    b0c = np.ascontiguousarray(b0.reshape(32, 128).T)
    bqc = np.ascontiguousarray(bq_sel.reshape(8, 128).T)

    sel = _mask_sel()

    nc = build_program()

    in_maps = []
    for c in range(N_CORES):
        t0, t1 = c * TPC, (c + 1) * TPC
        xt_c = np.ascontiguousarray(xt_full[:, t0:t1])
        blk = sel[t0:t1, :]
        m4 = np.empty((OT, 4, 128, TPC), dtype=ml_dtypes.bfloat16)
        for l in range(4):
            arr = (blk == l).T.reshape(OT, 128, TPC)
            m4[:, l] = (arr.astype(np.float32) * 4.0).astype(ml_dtypes.bfloat16)
        in_maps.append({
            "xt": xt_c, "w0t": w0t, "wqt": wqt, "wot": wot,
            "at": at_h, "bt": bt_h, "m": M, "zbd": zbd_h, "eh": eh_h,
            "id128": np.eye(128, dtype=np.float32),
            "b0c": b0c, "bqc": bqc, "m4": m4,
        })

    res = run_bass_kernel_spmd(nc, in_maps, core_ids=list(range(N_CORES)))
    outs = [np.asarray(r["out"]) for r in res.results]
    full = np.concatenate(outs, axis=1)
    return np.ascontiguousarray(full.T).reshape(4, 4096, 4096)


# version 07
# speedup vs baseline: 1.2553x; 1.1589x over previous
import os
import numpy as np
import ml_dtypes

import concourse.bass as bass
import concourse.mybir as mybir
from concourse.bass_utils import run_bass_kernel_spmd

F32 = mybir.dt.float32
BF16 = mybir.dt.bfloat16
AF = mybir.ActivationFunctionType
ALU = mybir.AluOpType

N_CORES = 8
D = 4096
KT = 32
TPC = 2048
NCH = 4
T = 512
OT = 32
HKV = 8
ENGS = ("tensor", "scalar", "vector", "sync", "gpsimd")


def _mask_sel():
    import jax
    import jax.numpy as jnp

    use64 = bool(os.environ.get("BASS_SIM_FP64"))
    if use64:
        jax.config.update("jax_enable_x64", True)
    cpu = jax.devices("cpu")[0]
    with jax.default_device(cpu):
        key = jax.random.key(42, impl="threefry2x32")
        dt = jnp.float64 if use64 else jnp.float32
        u = np.asarray(jax.random.uniform(key, (4, 4096, 4096), dtype=dt))
    sel = (u * 4).astype(np.uint8)
    return sel.reshape(16384, 4096)


class Sched:

    def __init__(self, sems):
        self.sems = sems
        self.prog = {e: [] for e in ENGS}
        self.cnt = {k: 0 for k in sems}
        self.last_wait = {e: {k: 0 for k in sems} for e in ENGS}

    def wait(self, eng, sem, val):
        if val <= 0:
            return
        assert val <= self.cnt[sem], f"wait {sem}>={val} > produced {self.cnt[sem]}"
        if self.last_wait[eng][sem] >= val:
            return
        self.last_wait[eng][sem] = val
        h = self.sems[sem]
        self.prog[eng].append(lambda e, h=h, v=val: e.wait_ge(h, v))

    def inc(self, sem, by):
        self.cnt[sem] += by
        return self.cnt[sem]

    def emit(self, eng, make, sem=None, by=1, fence=None):
        if eng in ("vector", "scalar"):
            self.prog[eng].append(lambda e: e.drain())
        if sem is None:
            self.prog[eng].append(make)
            return None
        n = self.inc(sem, by)
        h = self.sems[sem]
        self.prog[eng].append(lambda e, h=h, b=by: make(e).then_inc(h, b))
        if fence is None:
            fence = sem in ("qS", "qG")
        if fence and sem in ("qS", "qG"):
            self.wait(eng, sem, n)
        return n


def build_program():
    nc = bass.Bass("TRN2", target_bir_lowering=False, debug=False)

    xt = nc.dram_tensor("xt", [D, TPC], F32, kind="ExternalInput")
    w0t = nc.dram_tensor("w0t", [D, D], F32, kind="ExternalInput")
    wqt = nc.dram_tensor("wqt", [D, 1024], F32, kind="ExternalInput")
    wot = nc.dram_tensor("wot", [D, D], F32, kind="ExternalInput")
    at_d = nc.dram_tensor("at", [D, 128], F32, kind="ExternalInput")
    bt_d = nc.dram_tensor("bt", [128, D], F32, kind="ExternalInput")
    m_d = nc.dram_tensor("m", [HKV, 128, 128], F32, kind="ExternalInput")
    zbd_d = nc.dram_tensor("zbd", [128, HKV * 8], F32, kind="ExternalInput")
    eh_d = nc.dram_tensor("eh", [HKV, HKV * 128], F32, kind="ExternalInput")
    b0c_d = nc.dram_tensor("b0c", [128, 32], F32, kind="ExternalInput")
    bqc_d = nc.dram_tensor("bqc", [128, 8], F32, kind="ExternalInput")
    m4_d = nc.dram_tensor("m4", [OT, NCH, 128, 4 * T], BF16, kind="ExternalInput")
    out_d = nc.dram_tensor("out", [D, TPC], F32, kind="ExternalOutput")
    w0s = nc.dram_tensor("w0s", [OT, 128, KT * 128], BF16, kind="Internal")
    wqs = nc.dram_tensor("wqs", [HKV, 128, KT * 128], BF16, kind="Internal")
    wos = nc.dram_tensor("wos", [OT, 128, HKV * 128], BF16, kind="Internal")

    RS = nc.alloc_sbuf_tensor("RS", [128, KT * T], F32)
    XN = nc.alloc_sbuf_tensor("XN", [128, KT * T], BF16)
    WS = [nc.alloc_sbuf_tensor(f"WS{i}", [128, KT * 128], BF16) for i in range(2)]
    MS = [nc.alloc_sbuf_tensor(f"MS{i}", [128, 4 * T], BF16) for i in range(2)]
    WF = [nc.alloc_sbuf_tensor(f"WF{i}", [128, 2048], F32) for i in range(2)]
    AT = nc.alloc_sbuf_tensor("AT", [128, KT * 128], BF16)
    BTs = nc.alloc_sbuf_tensor("BTs", [128, D], BF16)
    HB = nc.alloc_sbuf_tensor("HB", [128, T], BF16)
    X2G = [nc.alloc_sbuf_tensor(f"X2G{i}", [128, 8 * T], BF16) for i in range(2)]
    X2 = [X2G[0][:, 0:T], X2G[0][:, T:2 * T]]
    EQ = X2G[1]
    YS = [nc.alloc_sbuf_tensor(f"YS{i}", [128, 4 * T], BF16) for i in range(2)]
    BS = [nc.alloc_sbuf_tensor(f"BS{i}", [128, T], BF16) for i in range(2)]
    RRb = nc.alloc_sbuf_tensor("RRb", [128, T], BF16)
    MSB = nc.alloc_sbuf_tensor("MSB", [128, HKV * 128], BF16)
    ZBD = nc.alloc_sbuf_tensor("ZBD", [128, HKV * 8], BF16)
    EH = nc.alloc_sbuf_tensor("EH", [HKV, HKV * 128], F32)
    DEN = nc.alloc_sbuf_tensor("DEN", [HKV, T], F32)
    RDEN = nc.alloc_sbuf_tensor("RDEN", [HKV, T], F32)
    STAT = nc.alloc_sbuf_tensor("STAT", [1, 4 * T], F32)
    RSTD = nc.alloc_sbuf_tensor("RSTD", [1, T], F32)
    ONE128B = nc.alloc_sbuf_tensor("ONE128B", [128, 1], BF16)
    ONEF32 = nc.alloc_sbuf_tensor("ONEF32", [1, 128], F32)
    B0H = nc.alloc_sbuf_tensor("B0H", [128, 32], F32)
    BQC = nc.alloc_sbuf_tensor("BQC", [128, 8], F32)
    BQN = nc.alloc_sbuf_tensor("BQN", [128, 8], F32)
    TACC = nc.alloc_sbuf_tensor("TACC", [128, 4 * T], BF16)
    TMUL = nc.alloc_sbuf_tensor("TMUL", [128, 2 * T], BF16)

    P0 = nc.alloc_psum_tensor("P0", [128, T], F32)
    PY = nc.alloc_psum_tensor("PY", [128, 4 * T], F32)
    PH = nc.alloc_psum_tensor("PH", [128, T], F32)
    PB = nc.alloc_psum_tensor("PB", [128, T], F32)
    PC = nc.alloc_psum_tensor("PC", [128, T], F32)

    sems = {n: nc.alloc_semaphore(n) for n in ("qS", "qG", "tP", "tA", "tV")}
    S = Sched(sems)
    AP = bass.AP

    pfree = {k: ("tV", 0) for k in ("P0", "PY", "PH", "PB", "PC")}

    def pe_psum(i):
        sem, val = pfree[i]
        S.wait("tensor", sem, val)

    id_d = nc.dram_tensor("id128", [128, 128], F32, kind="ExternalInput")
    IDB = nc.alloc_sbuf_tensor("IDB", [128, 128], BF16)

    st = dict(
        last_rs_tv=[0] * KT,
        last_rs_qg=[0] * KT,
        xn_tv=[0] * KT,
        xn_read_tp=0,
        x2_tv=0,
        x2_pe=[0, 0],
        ws_pe=[0, 0],
        ws_wb_qg={},
        ws_wb_par=[0, 0],
        ms_free=[("tV", 0), ("tV", 0)],
        wf_ta=[0, 0],
        bpr_free_tv=[0, 0],
        eq_read_tp=0,
        eluq_wait=("tV", 0),
    )

    S.emit("vector", lambda e: e.memset(ONE128B[:, :], 1.0))
    S.emit("vector", lambda e: e.memset(ONEF32[:, :], 1.0))

    S.emit("gpsimd", lambda e: e.dma_start(out=B0H[:, :], in_=b0c_d[:]), sem="qG", by=16)
    S.emit("gpsimd", lambda e: e.dma_start(out=BQC[:, :], in_=bqc_d[:]), sem="qG", by=16)
    S.emit("gpsimd", lambda e: e.dma_start(out=EH[:, :], in_=eh_d[:]), sem="qG", by=16)
    g_zt = S.emit("gpsimd", lambda e: e.dma_start(out=WF[1][:, 0:HKV * 8], in_=zbd_d[:]),
                  sem="qG", by=16)
    ta = 0
    for hf in range(2):
        src = AP(at_d, hf * 16 * 128 * 128, [[128, 128], [128 * 128, 16], [1, 128]])
        dst = WF[0][:, 0:2048].rearrange("p (k f) -> p k f", f=128)
        S.wait("gpsimd", "tA", ta)
        qg = S.emit("gpsimd", lambda e, s=src, d=dst: e.dma_start(out=d, in_=s),
                    sem="qG", by=16)
        S.wait("scalar", "qG", qg)
        ta = S.emit("scalar", lambda e, hf=hf: e.copy(
            out=AT[:, hf * 2048:(hf + 1) * 2048], in_=WF[0][:, 0:2048]), sem="tA")
    S.wait("scalar", "qG", g_zt)
    ta = S.emit("scalar", lambda e: e.copy(
        out=ZBD[:, :], in_=WF[1][:, 0:HKV * 8]), sem="tA")
    for hf in range(2):
        src = AP(bt_d, hf * 2048, [[D, 128], [1, 2048]])
        S.wait("gpsimd", "tA", ta)
        qg = S.emit("gpsimd", lambda e, s=src: e.dma_start(out=WF[1][:, 0:2048], in_=s),
                    sem="qG", by=16)
        S.wait("scalar", "qG", qg)
        ta = S.emit("scalar", lambda e, hf=hf: e.copy(
            out=BTs[:, hf * 2048:(hf + 1) * 2048], in_=WF[1][:, 0:2048]), sem="tA")
    for h in range(HKV):
        src = AP(m_d, h * 128 * 128, [[128, 128], [1, 128]])
        S.wait("gpsimd", "tA", ta)
        qg = S.emit("gpsimd", lambda e, s=src: e.dma_start(out=WF[0][:, 0:128], in_=s),
                    sem="qG", by=16)
        S.wait("scalar", "qG", qg)
        ta = S.emit("scalar", lambda e, h=h: e.copy(
            out=MSB[:, h * 128:(h + 1) * 128], in_=WF[0][:, 0:128]), sem="tA")
    S.wait("gpsimd", "tA", ta)
    qg = S.emit("gpsimd", lambda e: e.dma_start(out=WF[0][:, 0:128], in_=id_d[:]),
                sem="qG", by=16)
    S.wait("scalar", "qG", qg)
    ta = S.emit("scalar", lambda e: e.copy(out=IDB[:, :], in_=WF[0][:, 0:128]), sem="tA")
    st["wf_ta"] = [ta, ta]
    S.wait("vector", "qG", g_zt)
    S.wait("tensor", "qG", g_zt)
    S.emit("vector", lambda e: e.tensor_scalar_mul(B0H[:, :], B0H[:, :], 0.5))
    setup_tv = S.emit("vector", lambda e: e.tensor_scalar_mul(
        BQN[:, :], BQC[:, :], -1.0), sem="tV")
    S.wait("tensor", "tV", setup_tv)
    S.wait("scalar", "tV", setup_tv)
    S.wait("sync", "tV", setup_tv)
    S.wait("gpsimd", "tV", setup_tv)

    def ln_phase(chunk, phase):
        G = 8
        NG = KT // G
        pe_psum("PH")
        pe_psum("PB")
        stat_tp = 0
        S.wait("scalar", *st["eluq_wait"])
        S.wait("scalar", "tP", st["xn_read_tp"])
        for g in range(NG):
            lo, hi = g * G * T, (g + 1) * G * T
            if phase == 0:
                S.wait("scalar", "qG", st["last_rs_qg"][KT - 1])
            else:
                for k in range(g * G, (g + 1) * G):
                    S.wait("scalar", "tV", st["last_rs_tv"][k])
            S.emit("scalar", lambda e, lo=lo, hi=hi: e.copy(
                out=XN[:, lo:hi], in_=RS[:, lo:hi]))
            S.wait("scalar", "tP", st["x2_pe"][g % 2])
            ta = S.emit("scalar", lambda e, g=g, lo=lo, hi=hi: e.activation(
                out=X2G[g % 2][:, :], in_=XN[:, lo:hi], func=AF.Square), sem="tA")
            S.wait("tensor", "tA", ta)
            for k in range(G):
                kk = g * G + k
                S.emit("tensor", lambda e, kk=kk: e.matmul(
                    PH[0:1, :], ONE128B[:, :], XN[:, kk * T:(kk + 1) * T],
                    start=(kk == 0), stop=(kk == KT - 1)))
                stat_tp = S.emit("tensor", lambda e, g=g, k=k, kk=kk: e.matmul(
                    PB[0:1, :], ONE128B[:, :], X2G[g % 2][:, k * T:(k + 1) * T],
                    start=(kk == 0), stop=(kk == KT - 1)),
                    sem=("tP" if k == G - 1 else None))
            st["x2_pe"][g % 2] = stat_tp
        S.wait("vector", "tP", stat_tp)
        MU, EX2, VAR, RVAR = (STAT[0:1, i * T:(i + 1) * T] for i in range(4))
        S.emit("vector", lambda e: e.tensor_scalar_mul(MU, PH[0:1, :], 1.0 / D))
        S.emit("vector", lambda e: e.tensor_scalar_mul(EX2, PB[0:1, :], 1.0 / D))
        S.emit("vector", lambda e: e.tensor_mul(VAR, MU, MU))
        S.emit("vector", lambda e: e.tensor_sub(VAR, EX2, VAR))
        S.emit("vector", lambda e: e.tensor_scalar_add(VAR, VAR, 1e-5))
        v1 = S.emit("vector", lambda e: e.reciprocal(RVAR, VAR), sem="tV")
        pfree["PH"] = ("tV", v1)
        pfree["PB"] = ("tV", v1)
        S.wait("scalar", "tV", v1)
        a1 = S.emit("scalar", lambda e: e.activation(
            out=RSTD[0:1, :], in_=RVAR, func=AF.Sqrt), sem="tA")
        S.wait("tensor", "tA", a1)
        pe_psum("PY")
        S.emit("tensor", lambda e: e.matmul(
            PY[:, 0:T], ONEF32[:, :], MU, start=True, stop=True))
        rep_tp = S.emit("tensor", lambda e: e.matmul(
            PY[:, T:2 * T], ONEF32[:, :], RSTD[0:1, :], start=True, stop=True), sem="tP")
        S.wait("vector", "tP", rep_tp)
        S.wait("vector", "tA", ta)
        mub = AP(PY, 0, [[4 * T, 128], [0, G], [1, T]])
        rsb = AP(PY, T, [[4 * T, 128], [0, G], [1, T]])
        for g in range(NG):
            lo, hi = g * G * T, (g + 1) * G * T
            xg = XN[:, lo:hi].rearrange("p (g t) -> p g t", t=T)
            x2g = X2G[g % 2][:, :].rearrange("p (g t) -> p g t", t=T)
            S.emit("vector", lambda e, x2g=x2g, xg=xg, mub=mub: e.tensor_tensor(
                x2g, xg, mub, ALU.subtract))
            tv = S.emit("vector", lambda e, xg=xg, x2g=x2g, rsb=rsb: e.tensor_tensor(
                xg, x2g, rsb, ALU.mult), sem="tV")
            for k in range(g * G, (g + 1) * G):
                st["xn_tv"][k] = tv
        st["x2_tv"] = st["xn_tv"][KT - 1]
        st["eluq_wait"] = ("tV", st["x2_tv"])
        pfree["PY"] = ("tV", st["xn_tv"][KT - 1])

    def stream_strip(kind, first, idx, par, nkt):
        scratch = {"w0": w0s, "wq": wqs, "wo": wos}[kind]
        if first:
            if kind == "wo":
                for hf in range(2):
                    src = AP(wot, (hf * 16 * 128) * D + idx * 128,
                             [[D, 128], [128 * D, 16], [1, 128]])
                    dst = WF[hf][:, 0:2048].rearrange("p (k f) -> p k f", f=128)
                    S.wait("sync", "tA", st["wf_ta"][hf])
                    qs = S.emit("sync", lambda e, s=src, d=dst: e.dma_start(
                        out=d, in_=s), sem="qS", by=16)
                    S.wait("scalar", "qS", qs)
                    S.wait("scalar", *st["ms_free"][hf])
                    ta = S.emit("scalar", lambda e, hf=hf: e.copy(
                        out=MS[hf][:, 0:2048], in_=WF[hf][:, 0:2048]), sem="tA")
                    st["wf_ta"][hf] = ta
                S.wait("tensor", "tA", ta)
                pe_psum("PY")
                fold_tp = 0
                for j in range(HKV):
                    for g in range(4):
                        b = 4 * j + g
                        fold_tp = S.emit("tensor", lambda e, j=j, g=g, b=b: e.matmul(
                            PY[:, (j // 4) * T + (j % 4) * 128:(j // 4) * T + (j % 4 + 1) * 128],
                            IDB[:, :],
                            MS[b // 16][:, (b % 16) * 128:(b % 16 + 1) * 128],
                            start=(g == 0), stop=(g == 3)),
                            sem=("tP" if (j == HKV - 1 and g == 3) else None))
                st["ms_free"][0] = ("tP", fold_tp)
                st["ms_free"][1] = ("tP", fold_tp)
                S.wait("scalar", "tP", fold_tp)
                S.wait("scalar", "tP", st["ws_pe"][par])
                S.wait("scalar", "qG", st["ws_wb_par"][par])
                S.emit("scalar", lambda e, par=par: e.copy(
                    out=WS[par][:, 0:512], in_=PY[:, 0:T]))
                ta = S.emit("scalar", lambda e, par=par: e.copy(
                    out=WS[par][:, 512:1024], in_=PY[:, T:2 * T]), sem="tA")
                pfree["PY"] = ("tA", ta)
            else:
                src_t, width = (w0t, D) if kind == "w0" else (wqt, 1024)
                ta = 0
                for hf in range(2):
                    src = AP(src_t, (hf * 16 * 128) * width + idx * 128,
                             [[width, 128], [128 * width, 16], [1, 128]])
                    dst = WF[hf][:, 0:2048].rearrange("p (k f) -> p k f", f=128)
                    S.wait("sync", "tA", st["wf_ta"][hf])
                    qs = S.emit("sync", lambda e, s=src, d=dst: e.dma_start(
                        out=d, in_=s), sem="qS", by=16)
                    S.wait("scalar", "qS", qs)
                    if hf == 0:
                        S.wait("scalar", "tP", st["ws_pe"][par])
                        S.wait("scalar", "qG", st["ws_wb_par"][par])
                    scl = 0.5 if kind == "w0" else 1.0
                    ta = S.emit("scalar", lambda e, hf=hf, par=par, scl=scl: e.activation(
                        out=WS[par][:, hf * 2048:(hf + 1) * 2048],
                        in_=WF[hf][:, 0:2048], func=AF.Identity, scale=scl), sem="tA")
                    st["wf_ta"][hf] = ta
            S.wait("gpsimd", "tA", ta)
            dstw = AP(scratch, idx * nkt * 128 * 128,
                      [[nkt * 128, 128], [1, nkt * 128]])
            srcw = WS[par][:, 0:nkt * 128]
            wb = S.emit("gpsimd", lambda e, d=dstw, s=srcw: e.dma_start(out=d, in_=s),
                        sem="qG", by=16)
            st["ws_wb_qg"][(kind, idx)] = wb
            st["ws_wb_par"][par] = wb
            return ("tA", ta)
        else:
            S.wait("sync", "tP", st["ws_pe"][par])
            S.wait("sync", "qG", max(st["ws_wb_qg"].get((kind, idx), 0),
                                     st["ws_wb_par"][par]))
            src = AP(scratch, idx * nkt * 128 * 128,
                     [[nkt * 128, 128], [1, nkt * 128]])
            dst = WS[par][:, 0:nkt * 128]
            qs = S.emit("sync", lambda e, s=src, d=dst: e.dma_start(out=d, in_=s),
                        sem="qS", by=16)
            return ("qS", qs)

    def ffn_phase(chunk, call):
        pe_psum("PH")
        h_tp = 0
        for k in range(KT):
            S.wait("tensor", "tV", st["xn_tv"][k])
            h_tp = S.emit("tensor", lambda e, k=k: e.matmul(
                PH[:, :], AT[:, k * 128:(k + 1) * 128], XN[:, k * T:(k + 1) * T],
                start=(k == 0), stop=(k == KT - 1)), sem="tP")
        S.wait("scalar", "tP", h_tp)
        hb_ta = S.emit("scalar", lambda e: e.copy(out=HB[:, :], in_=PH[:, :]), sem="tA")
        pfree["PH"] = ("tA", hb_ta)

        firstpass = (chunk == 0 and call == 1)
        for o in range(OT):
            par = o % 2
            sem, val = stream_strip("w0", firstpass, o, par, KT)
            S.wait("gpsimd", *st["ms_free"][par])
            msrc = AP(m4_d, ((o * NCH + chunk) * 128) * (4 * T),
                      [[4 * T, 128], [1, 4 * T]])
            mq = S.emit("gpsimd", lambda e, s=msrc, par=par: e.dma_start(
                out=MS[par][:, :], in_=s), sem="qG", by=16)
            S.wait("tensor", sem, val)
            pe_psum("P0")
            pe_psum("PY")
            S.wait("tensor", "tA", hb_ta)
            for k in range(KT):
                S.emit("tensor", lambda e, k=k, par=par: e.matmul(
                    P0[:, :], WS[par][:, k * 128:(k + 1) * 128],
                    XN[:, k * T:(k + 1) * T],
                    start=(k == 0), stop=(k == KT - 1)))
            pe_o = 0
            for l in range(4):
                pe_o = S.emit("tensor", lambda e, l=l, o=o: e.matmul(
                    PY[:, l * T:(l + 1) * T],
                    BTs[32 * l:32 * (l + 1), o * 128:(o + 1) * 128],
                    HB[32 * l:32 * (l + 1), :],
                    start=True, stop=True, tile_position=(32 * l, 0)),
                    sem=("tP" if l == 3 else None))
            st["ws_pe"][par] = pe_o
            if o == OT - 1:
                st["xn_read_tp"] = pe_o
            S.wait("scalar", "tP", pe_o)
            S.wait("scalar", "tV", st["bpr_free_tv"][par])
            S.emit("scalar", lambda e, o=o, par=par: e.activation(
                out=BS[par][:, :], in_=P0[:, :], func=AF.Identity,
                bias=B0H[:, o:o + 1]))
            evac_ta = S.emit("scalar", lambda e, par=par: e.copy(
                out=YS[par][:, :], in_=PY[:, :]), sem="tA")
            pfree["P0"] = ("tA", evac_ta)
            pfree["PY"] = ("tA", evac_ta)
            S.wait("vector", "tA", evac_ta)
            S.wait("vector", "qG", mq)
            S.emit("vector", lambda e, par=par: e.tensor_mul(
                TACC[:, :], YS[par][:, :], MS[par][:, :]))
            S.emit("vector", lambda e: e.tensor_add(
                TMUL[:, :], TACC[:, 0:2 * T], TACC[:, 2 * T:4 * T]))
            S.emit("vector", lambda e: e.tensor_add(
                TACC[:, 0:T], TMUL[:, 0:T], TMUL[:, T:2 * T]))
            S.emit("vector", lambda e, par=par: e.tensor_add(
                TACC[:, T:2 * T], TACC[:, 0:T], BS[par][:, :]))
            tv = S.emit("vector", lambda e, o=o: e.tensor_add(
                RS[:, o * T:(o + 1) * T], RS[:, o * T:(o + 1) * T],
                TACC[:, T:2 * T]), sem="tV")
            st["ms_free"][par] = ("tV", tv)
            st["bpr_free_tv"][par] = tv
            st["last_rs_tv"][o] = tv
            if call == 2:
                S.wait("gpsimd", "tV", tv)
                dsto = AP(out_d, (o * 128) * TPC + chunk * T, [[TPC, 128], [1, T]])
                S.emit("gpsimd", lambda e, d=dsto, o=o: e.dma_start(
                    out=d, in_=RS[:, o * T:(o + 1) * T]), sem="qG", by=16)

    def attn_phase(chunk):
        first = (chunk == 0)
        eluq_tv = 0
        for hh in range(HKV):
            par = hh % 2
            sem, val = stream_strip("wq", first, hh, par, KT)
            QP = P0 if hh % 2 == 0 else PH
            qkey = "P0" if hh % 2 == 0 else "PH"
            S.wait("tensor", sem, val)
            pe_psum(qkey)
            q_tp = 0
            for k in range(KT):
                S.wait("tensor", "tV", st["xn_tv"][k])
                q_tp = S.emit("tensor", lambda e, k=k, par=par, QP=QP: e.matmul(
                    QP[:, :], WS[par][:, k * 128:(k + 1) * 128],
                    XN[:, k * T:(k + 1) * T],
                    start=(k == 0), stop=(k == KT - 1)),
                    sem=("tP" if k == KT - 1 else None))
            st["ws_pe"][par] = q_tp
            if hh == HKV - 1:
                st["xn_read_tp"] = q_tp
            S.wait("scalar", "tP", q_tp)
            S.wait("scalar", *st["eluq_wait"])
            S.emit("scalar", lambda e, hh=hh, QP=QP: e.activation(
                out=X2[0][:, :], in_=QP[:, :], func=AF.Relu,
                bias=BQN[:, hh:hh + 1], scale=-1.0))
            S.emit("scalar", lambda e: e.activation(
                out=X2[1][:, :], in_=X2[0][:, :], func=AF.Exp, scale=-1.0))
            elu_ta = S.emit("scalar", lambda e, hh=hh, QP=QP: e.activation(
                out=X2[0][:, :], in_=QP[:, :], func=AF.Relu,
                bias=BQC[:, hh:hh + 1]), sem="tA")
            pfree[qkey] = ("tA", elu_ta)
            S.wait("vector", "tA", elu_ta)
            if hh == 0:
                S.wait("vector", "tP", st["eq_read_tp"])
            eluq_tv = S.emit("vector", lambda e, hh=hh: e.tensor_add(
                EQ[:, hh * T:(hh + 1) * T], X2[1][:, :], X2[0][:, :]), sem="tV")
            st["eluq_wait"] = ("tV", eluq_tv)
        st["x2_tv"] = eluq_tv
        S.wait("tensor", "tV", eluq_tv)
        pe_psum("PC")
        den_tp = 0
        for h in range(HKV):
            den_tp = S.emit("tensor", lambda e, h=h: e.matmul(
                PC[0:HKV, :], ZBD[:, h * 8:(h + 1) * 8], EQ[:, h * T:(h + 1) * T],
                start=(h == 0), stop=(h == HKV - 1)),
                sem=("tP" if h == HKV - 1 else None))
        S.wait("vector", "tP", den_tp)
        S.emit("vector", lambda e: e.tensor_scalar(
            DEN[:, :], PC[0:HKV, :], 1e-6, None, op0=ALU.add))
        rden_tv = S.emit("vector", lambda e: e.reciprocal(RDEN[:, :], DEN[:, :]), sem="tV")
        pfree["PC"] = ("tV", rden_tv)
        retr_tv = 0
        for h in range(HKV):
            pe_psum("PB")
            pe_psum("PC")
            S.emit("tensor", lambda e, h=h: e.matmul(
                PB[:, :], MSB[:, h * 128:(h + 1) * 128], EQ[:, h * T:(h + 1) * T],
                start=True, stop=True))
            S.wait("tensor", "tV", rden_tv)
            num_tp = S.emit("tensor", lambda e, h=h: e.matmul(
                PC[:, :], EH[:, h * 128:(h + 1) * 128], RDEN[:, :],
                start=True, stop=True), sem="tP")
            S.wait("scalar", "tP", num_tp)
            rrep_ta = S.emit("scalar", lambda e: e.copy(out=RRb[:, :], in_=PC[:, :]),
                             sem="tA")
            pfree["PC"] = ("tA", rrep_ta)
            S.wait("vector", "tP", num_tp)
            S.wait("vector", "tA", rrep_ta)
            retr_tv = S.emit("vector", lambda e, h=h: e.tensor_mul(
                EQ[:, h * T:(h + 1) * T], PB[:, :], RRb[:, :]), sem="tV")
            pfree["PB"] = ("tV", retr_tv)
        S.wait("tensor", "tV", retr_tv)
        wo_tp = 0
        for o in range(OT):
            par = o % 2
            sem, val = stream_strip("wo", first, o, par, HKV)
            WP = P0 if o % 2 == 0 else PH
            wkey = "P0" if o % 2 == 0 else "PH"
            S.wait("tensor", sem, val)
            pe_psum(wkey)
            for k in range(HKV):
                wo_tp = S.emit("tensor", lambda e, k=k, par=par, WP=WP: e.matmul(
                    WP[:, :], WS[par][:, k * 128:(k + 1) * 128],
                    EQ[:, k * T:(k + 1) * T],
                    start=(k == 0), stop=(k == HKV - 1)),
                    sem=("tP" if k == HKV - 1 else None))
            st["ws_pe"][par] = wo_tp
            S.wait("vector", "tP", wo_tp)
            tv = S.emit("vector", lambda e, o=o, WP=WP: e.tensor_add(
                RS[:, o * T:(o + 1) * T], RS[:, o * T:(o + 1) * T], WP[:, :]),
                sem="tV")
            pfree[wkey] = ("tV", tv)
            st["last_rs_tv"][o] = tv
        st["eq_read_tp"] = wo_tp
        st["xn_read_tp"] = max(st["xn_read_tp"], wo_tp)

    stop = globals().get("DEBUG_STOP")
    nch = globals().get("DEBUG_NCH", NCH)
    done = False
    for c in range(nch):
        if done:
            break
        for k in range(KT):
            src = AP(xt, (k * 128) * TPC + c * T, [[TPC, 128], [1, T]])
            st["last_rs_qg"][k] = S.emit("gpsimd", lambda e, s=src, k=k: e.dma_start(
                out=RS[:, k * T:(k + 1) * T], in_=s), sem="qG", by=16,
                fence=(k == KT - 1))
        for name, fn in (("ln1", lambda: ln_phase(c, 0)),
                         ("ffn1", lambda: ffn_phase(c, 1)),
                         ("ln2", lambda: ln_phase(c, 1)),
                         ("attn", lambda: attn_phase(c)),
                         ("ln3", lambda: ln_phase(c, 2)),
                         ("ffn2", lambda: ffn_phase(c, 2))):
            fn()
            if stop == name:
                done = True
                break

    if stop is not None:
        S.wait("gpsimd", "tP", S.cnt["tP"])
        S.wait("gpsimd", "tA", S.cnt["tA"])
        S.wait("gpsimd", "tV", S.cnt["tV"])
        for k in range(KT):
            dsto = AP(out_d, (k * 128) * TPC, [[TPC, 128], [1, T]])
            S.emit("gpsimd", lambda e, d=dsto, k=k: e.dma_start(
                out=d, in_=RS[:, k * T:(k + 1) * T]), sem="qG", by=16,
                fence=(k == KT - 1))
        for k in range(KT):
            dsto = AP(out_d, (k * 128) * TPC + T, [[TPC, 128], [1, T // 2]])
            S.emit("gpsimd", lambda e, d=dsto, k=k: e.dma_start(
                out=d, in_=XN[:, k * T:(k + 1) * T].bitcast(F32)), sem="qG", by=16,
                fence=(k == KT - 1))
        S.emit("gpsimd", lambda e: e.dma_start(
            out=AP(out_d, 300 * TPC, [[TPC, 1], [1, T]]), in_=STAT[0:1, 0:T]), sem="qG", by=16)
        S.emit("gpsimd", lambda e: e.dma_start(
            out=AP(out_d, 301 * TPC, [[TPC, 1], [1, T]]), in_=RSTD[0:1, :]), sem="qG", by=16)
        S.emit("gpsimd", lambda e: e.dma_start(
            out=AP(out_d, 302 * TPC, [[TPC, 1], [1, T]]),
            in_=STAT[0:1, 3 * T:4 * T]), sem="qG", by=16)

    S.wait("gpsimd", "qG", S.cnt["qG"])

    with nc.Block() as block:
        @block.tensor
        def _(e):
            for f in S.prog["tensor"]:
                f(e)

        @block.scalar
        def _(e):
            for f in S.prog["scalar"]:
                f(e)

        @block.vector
        def _(e):
            for f in S.prog["vector"]:
                f(e)

        @block.sync
        def _(e):
            for f in S.prog["sync"]:
                f(e)

        @block.gpsimd
        def _(e):
            for f in S.prog["gpsimd"]:
                f(e)

    return nc


def kernel(L0_attn_hidden_states, L0_attn_Wq, L0_attn_bq, L0_attn_Wk, L0_attn_bk,
           L0_attn_Wv, L0_attn_bv, L0_attn_Wo, L0_attn_M, L0_attn_z,
           L0_ffn_W0, L0_ffn_b0, L0_ffn_A, L0_ffn_B):
    x = np.asarray(L0_attn_hidden_states, dtype=np.float32)
    Wq = np.asarray(L0_attn_Wq, dtype=np.float32)
    bq = np.asarray(L0_attn_bq, dtype=np.float32)
    Wo = np.asarray(L0_attn_Wo, dtype=np.float32)
    M = np.asarray(L0_attn_M, dtype=np.float32)
    z = np.asarray(L0_attn_z, dtype=np.float32)
    W0 = np.asarray(L0_ffn_W0, dtype=np.float32)
    b0 = np.asarray(L0_ffn_b0, dtype=np.float32)
    A = np.asarray(L0_ffn_A, dtype=np.float32)
    B = np.asarray(L0_ffn_B, dtype=np.float32)

    xt_full = np.ascontiguousarray(x.reshape(16384, D).T)
    w0t = np.ascontiguousarray(W0.T)
    sel_rows = np.concatenate([np.arange(h * 128, (h + 1) * 128)
                               for h in range(0, 32, 4)])
    wqt = np.ascontiguousarray(Wq[sel_rows].T)
    bq_sel = bq[sel_rows]
    wot = np.ascontiguousarray(Wo.T)
    atp = np.zeros((128, D), np.float32)
    for l in range(4):
        atp[32 * l:32 * l + 16] = A[l]
    at_h = np.ascontiguousarray(atp.T)
    btp = np.zeros((128, D), np.float32)
    for l in range(4):
        btp[32 * l:32 * l + 16] = B[l].T
    bt_h = np.ascontiguousarray(btp)
    zbd_h = np.zeros((128, HKV * 8), np.float32)
    for h in range(HKV):
        zbd_h[:, h * 8 + h] = z[h]
    eh_h = np.zeros((HKV, HKV * 128), np.float32)
    for h in range(HKV):
        eh_h[h, h * 128:(h + 1) * 128] = 1.0
    b0c = np.ascontiguousarray(b0.reshape(32, 128).T)
    bqc = np.ascontiguousarray(bq_sel.reshape(8, 128).T)

    sel = _mask_sel()

    nc = build_program()

    in_maps = []
    for c in range(N_CORES):
        t0, t1 = c * TPC, (c + 1) * TPC
        xt_c = np.ascontiguousarray(xt_full[:, t0:t1])
        blk = sel[t0:t1, :]
        m4 = np.empty((OT, NCH, 128, 4 * T), dtype=ml_dtypes.bfloat16)
        for l in range(4):
            arr = (blk == l).T.reshape(OT, 128, NCH, T).transpose(0, 2, 1, 3)
            m4[:, :, :, l * T:(l + 1) * T] = (arr.astype(np.float32) * 4.0
                                              ).astype(ml_dtypes.bfloat16)
        in_maps.append({
            "xt": xt_c, "w0t": w0t, "wqt": wqt, "wot": wot,
            "at": at_h, "bt": bt_h, "m": M, "zbd": zbd_h, "eh": eh_h,
            "id128": np.eye(128, dtype=np.float32),
            "b0c": b0c, "bqc": bqc, "m4": m4,
        })

    res = run_bass_kernel_spmd(nc, in_maps, core_ids=list(range(N_CORES)))
    outs = [np.asarray(r["out"]) for r in res.results]
    full = np.concatenate(outs, axis=1)
    return np.ascontiguousarray(full.T).reshape(4, 4096, 4096)
